# version 23
# baseline (speedup 1.0000x reference)
import contextlib
import numpy as np
import ml_dtypes

import concourse.bass as bass
import concourse.mybir as mybir
from concourse.bass_utils import run_bass_kernel_spmd

F32 = mybir.dt.float32
BF16 = mybir.dt.bfloat16
U8 = mybir.dt.uint8
AF = mybir.ActivationFunctionType
Alu = mybir.AluOpType
AX = mybir.AxisListType.X

NC_ = 8
T = 2048
D = 1024
N = 256
I = 2816
TC = T // 128
DCH = D // 128
NCH = N // 128
ICH = I // 128
NL = 2
EPS = 1e-5
SCAN_SEG = 2048
QTHR = [0.25, 0.75, 1.25, 1.75, 2.5, 3.5, 5.0]
KB = 1024

_DEBUG = []
_STAGE = 99
_SERIAL = False
_SERIAL2 = ''
_SERIAL3 = ''
_GEXTRA = ''
_ILW = True
_NOW = False
_DEBUG_SHAPES = {
    "hs0": ([128, NCH * T], F32),
    "cp0": ([128, NCH * T], F32),
    "hsp0": ([128, NCH * T], F32),
    "summ0": ([128, NC_ * 4], F32),
    "pk0": ([128, 4], F32),
    "gin0": ([128, 4], F32),
    "gout0": ([NC_ * 128, 4], F32),
    "hs1": ([128, NCH * T], F32),
    "hsp1": ([128, NCH * T], F32),
    "h1fm": ([128, DCH * T], F32),
    "x1": ([T, D], F32),
    "xq": ([128, DCH * T], F32),
    "pre2": ([T, D], F32),
    "hrow15": ([128, I], F32),
    "hq15": ([128, I], F32),
    "hqfm15": ([128, ICH * 128], F32),
    "sums2": ([128, TC], F32),
    "ssq2": ([128, TC], F32),
}


class Sem:
    def __init__(self, h):
        self.h = h
        self.n = 0

    def mark(self):
        return (self, self.n)


class Build:
    def __init__(self):
        self.q = {e: [] for e in ("sync", "scalar", "vector", "tensor", "gpsimd")}

    def step(self, eng, fn, waits=()):
        frozen = []
        for s in waits:
            if isinstance(s, tuple):
                if s[1] > 0:
                    frozen.append((s[0].h, s[1]))
            elif s.n > 0:
                frozen.append((s.h, s.n))

        def run(engine, _frozen=frozen, _fn=fn):
            for h, v in _frozen:
                engine.wait_ge(h, v)
            _fn(engine)

        self.q[eng].append(run)


def build_nc():
    nc = bass.Bass("TRN2", target_bir_lowering=False, debug=False)

    x_in = nc.dram_tensor("x", [T, D], F32, kind="ExternalInput").ap()
    wd_in = nc.dram_tensor("wd", [NL, D, N], F32, kind="ExternalInput").ap()
    wi_in = nc.dram_tensor("wi", [NL, D, N], F32, kind="ExternalInput").ap()
    wo_in = nc.dram_tensor("wo", [NL, N, D], F32, kind="ExternalInput").ap()
    ws_in = nc.dram_tensor("ws", [NL, N, D], F32, kind="ExternalInput").ap()
    alog_in = nc.dram_tensor("alog", [128, NCH, NL], F32, kind="ExternalInput").ap()
    bd_in = nc.dram_tensor("bd", [128, NCH, NL], F32, kind="ExternalInput").ap()
    bi_in = nc.dram_tensor("bi", [128, NCH, NL], F32, kind="ExternalInput").ap()
    bo_in = nc.dram_tensor("bo", [128, DCH, NL], F32, kind="ExternalInput").ap()
    bs_in = nc.dram_tensor("bs", [128, DCH, NL], F32, kind="ExternalInput").ap()
    borow_in = nc.dram_tensor("borow", [1, D], F32, kind="ExternalInput").ap()
    bsrow_in = nc.dram_tensor("bsrow", [1, D], F32, kind="ExternalInput").ap()
    wg_sl = nc.dram_tensor("wg_sl", [512, D], F32, kind="ExternalInput").ap()
    wu_sl = nc.dram_tensor("wu_sl", [512, D], F32, kind="ExternalInput").ap()
    wdn_sl = nc.dram_tensor("wdn_sl", [128, I], F32, kind="ExternalInput").ap()
    cpool = nc.dram_tensor("cpool", [128, 8], F32, kind="ExternalInput").ap()
    ident_in = nc.dram_tensor("ident", [128, 128], BF16, kind="ExternalInput").ap()
    ones1_in = nc.dram_tensor("ones1", [1, 128], BF16, kind="ExternalInput").ap()
    msel_in = nc.dram_tensor("msel", [128, 4, 8], F32, kind="ExternalInput").ap()
    ksel_in = nc.dram_tensor("ksel", [128, 4], F32, kind="ExternalInput").ap()

    out = nc.dram_tensor("out", [T, D], F32, kind="ExternalOutput").ap()
    dbg = {}
    for name in _DEBUG:
        shp, dt_ = _DEBUG_SHAPES[name]
        dbg[name] = nc.dram_tensor("dbg_" + name, shp, dt_, kind="ExternalOutput").ap()

    gin_wg = nc.dram_tensor("gin_wg", [128, DCH * 512], BF16).ap()
    gout_wg = nc.dram_tensor("gout_wg", [NC_ * 128, DCH * 512], BF16).ap()
    gin_wu = nc.dram_tensor("gin_wu", [128, DCH * 512], BF16).ap()
    gout_wu = nc.dram_tensor("gout_wu", [NC_ * 128, DCH * 512], BF16).ap()
    gin_wd = nc.dram_tensor("gin_wd", [I, 128], BF16).ap()
    gout_wd = nc.dram_tensor("gout_wd", [NC_ * I, 128], BF16).ap()
    wdq2 = nc.dram_tensor("wdq2", [I, D], BF16).ap()
    gin_s = [nc.dram_tensor(f"gin_s{l}", [128, 4], F32).ap() for l in range(NL)]
    gout_s = [nc.dram_tensor(f"gout_s{l}", [NC_ * 128, 4], F32).ap() for l in range(NL)]
    pre2_d = nc.dram_tensor("pre2_d", [T, D], BF16).ap()
    x1_d = nc.dram_tensor("x1_d", [T, D], BF16).ap()

    b = Build()

    with contextlib.ExitStack() as stack:
        SB = lambda *a, **k: stack.enter_context(nc.sbuf_tensor(*a, **k))
        PSUM = lambda *a: stack.enter_context(nc.psum_tensor(*a))

        arena = SB("arena", [128, 128 * KB], U8)

        def AV(off_kb, nbytes, shape, dt):
            v = arena[:, int(off_kb * KB):int(off_kb * KB) + int(nbytes)].bitcast(dt)
            n = int(np.prod(shape))
            assert v.shape[1] >= n, (shape, v.shape)
            v = v[:, 0:n]
            if len(shape) > 1:
                sh = " ".join(f"a{i}" for i in range(len(shape)))
                kw = {f"a{i}": s for i, s in enumerate(shape)}
                v = v.rearrange(f"p ({sh}) -> p {sh}", **kw)
            return v

        x_bf = AV(0, 32 * KB, [TC, D], BF16)
        a_tmp = AV(0, 8 * KB, [T], F32)
        a_dl = AV(8, 8 * KB, [T], F32)
        a_dc = AV(16, 8 * KB, [T], F32)
        a_dr = AV(24, 8 * KB, [T], F32)
        wgst = AV(0, 16 * KB, [2, DCH, 512], BF16)
        wust = AV(16, 16 * KB, [2, DCH, 512], BF16)
        x_fm = AV(32, 32 * KB, [DCH, T], BF16)
        n_au = AV(32, 4 * KB, [D], F32)
        n_k1 = AV(36, 2 * KB, [D], BF16)
        n_k2 = AV(38, 2 * KB, [D], BF16)
        n_t3 = AV(40, 2 * KB, [D], BF16)
        n_sg = AV(42, 2 * KB, [D], BF16)
        n_dst = AV(44, 2 * KB, [D], BF16)
        n_au2 = AV(46, 4 * KB, [D], F32)
        n_sg2 = AV(50, 2 * KB, [D], BF16)
        n_dst2 = AV(52, 2 * KB, [D], BF16)
        x1b = AV(54, 2 * KB, [D], BF16)
        w_k1 = AV(32, 5632, [I], BF16)
        w_k2 = AV(37.5, 5632, [I], BF16)
        w_t3 = AV(43, 5632, [I], BF16)
        w_sg = AV(48.5, 5632, [I], BF16)
        w_dst = AV(54, 5632, [I], BF16)
        w_dst2 = AV(88, 5632, [I], BF16)
        h1_fm = AV(64, 32 * KB, [DCH, T], BF16)
        xres = AV(64, 8 * KB, [2, D], F32)
        pre1 = AV(72, 4 * KB, [D], F32)
        x1f = AV(76, 4 * KB, [D], F32)
        w_au = AV(64, 11264, [I], F32)
        sqscr = AV(80, 2 * KB, [D], BF16)
        hq_fm = AV(82, 5632, [ICH, 128], BF16)
        pre2s = AV(94, 2 * KB, [D], BF16)
        pre2s2 = AV(61.5, 2 * KB, [D], BF16)
        x1rA = AV(76, 4 * KB, [2, D], BF16)
        x1rB = AV(59.5, 2 * KB, [D], BF16)
        x1r_s = lambda i: x1rA[:, i, :] if i < 2 else x1rB[:]
        pre2r = AV(72, 4 * KB, [2, D], BF16)
        outb = AV(64, 8 * KB, [2, D], F32)
        hs_sb = AV(96, 16 * KB, [NCH, T], F32)
        cp_sb = AV(112, 16 * KB, [NCH, T], F32)
        xq_fm = AV(96, 32 * KB, [DCH, T], BF16)

        wdwi = SB("wdwi", [128, 2, DCH, N], BF16)
        wo_sb = SB("wo_sb", [128, NCH, NL, D], BF16)
        ws_sb = SB("ws_sb", [128, NCH, NL, D], BF16)
        wdst = SB("wdst", [128, 3, D], BF16)
        d_au = SB("d_au", [128, D], F32)
        d_k1 = SB("d_k1", [128, D], BF16)
        d_k2 = SB("d_k2", [128, D], BF16)
        d_t3 = SB("d_t3", [128, D], BF16)
        d_sg = SB("d_sg", [128, D], BF16)
        wTst = SB("wTst_sb", [128, 8, 128], BF16)
        hrow = SB("hrow", [128, I], F32)
        hsp = SB("hsp", [128, NCH, T], BF16)
        zeros_b = SB("zeros_b", [128, T], BF16)
        q_bm = SB("q_bm", [128, 176], F32)
        q_rr = SB("q_rr", [128, 176], F32)
        q_sh = SB("q_sh", [128, 176], F32)
        cp_sb2 = SB("cpool_sb", [128, 8], F32)
        identb = SB("identb", [128, 128], BF16)
        ones1 = SB("ones1_sb", [1, 128], BF16)
        msel = SB("msel_sb", [128, 4, 8], F32)
        ksel = SB("ksel_sb", [128, 4], F32)
        alog_f = SB("alog_f", [128, NCH, NL], F32)
        sa_t = SB("sa_t", [128, NCH, NL], F32)
        nsa = SB("nsa", [128, NCH, NL], F32)
        bd_f = SB("bd_f", [128, NCH, NL], F32)
        bi_f = SB("bi_f", [128, NCH, NL], F32)
        bo_f = SB("bo_f", [128, DCH, NL], F32)
        bs_f = SB("bs_f", [128, DCH, NL], F32)
        summ = SB("summ", [128, NC_, 4], F32)
        pk_sb = SB("pk_sb", [128, 4], F32)
        jsel = SB("jsel", [128, NC_], F32)
        sj = SB("sj", [128, NCH, 5], F32)
        s4sel = SB("s4sel", [128, 4], F32)
        hin = SB("hin", [128, NCH], F32)
        st4b = SB("st4b", [128, NCH], BF16)
        bvec = SB("bvec", [128, DCH], F32)
        bvec0 = SB("bvec0", [128, DCH], F32)
        brow = SB("brow", [1, D], BF16)
        r1row = SB("r1row", [1, D], BF16)
        rowf = SB("rowf", [1, D], BF16)
        sums2 = SB("sums2", [128, TC], F32)
        ssq2 = SB("ssq2", [128, TC], F32)
        st1 = SB("st1", [128, 8], F32)
        scan_carry = SB("scan_carry", [128, 2], F32)

        CONSTS = [0.0, 1.0, -1.0, 0.5, -0.5, EPS]
        for i, v in enumerate(CONSTS):
            nc.const_aps.register(F32, v, cp_sb2[:, i:i + 1])

        psA = PSUM("psA", [128, 2048], F32)
        psB = PSUM("psB", [128, 1536], F32)
        psT = PSUM("psT", [128, 1024], BF16)

        sem_names = ("ld", "gp", "cc", "pe", "act", "vec", "st", "ws")
        S_ = {s: Sem(stack.enter_context(nc.semaphore(s))) for s in sem_names}
        ld, gp, cc, pe, act, vec, st, ws = (S_[k] for k in sem_names)
        sg_sl = [Sem(stack.enter_context(nc.semaphore(f"sgw{i}"))) for i in range(2)]
        su_sl = [Sem(stack.enter_context(nc.semaphore(f"suw{i}"))) for i in range(2)]
        sd_sl = [Sem(stack.enter_context(nc.semaphore(f"sdw{i}"))) for i in range(3)]
        sxr_sl = [Sem(stack.enter_context(nc.semaphore(f"sxr{i}"))) for i in range(2)]
        sx1_sl = [Sem(stack.enter_context(nc.semaphore(f"sx1{i}"))) for i in range(3)]
        sp2_sl = [Sem(stack.enter_context(nc.semaphore(f"sp2{i}"))) for i in range(2)]
        s_summ = Sem(stack.enter_context(nc.semaphore("s_summ")))
        s_wdwi = Sem(stack.enter_context(nc.semaphore("s_wdwi")))
        s_wh = Sem(stack.enter_context(nc.semaphore("s_wh")))
        s_x1b = Sem(stack.enter_context(nc.semaphore("s_x1b")))

        block = stack.enter_context(nc.Block())

        def dma(eng_name, dst, src, sem, waits=()):
            if _SERIAL or "dma" in _SERIAL2:
                waits = tuple(waits) + (pe, act, vec, ws, ld, gp, st)
            def f(e, _d=dst, _s=src, _sem=sem):
                e.dma_start(out=_d, in_=_s).then_inc(_sem.h, 16)
            b.step(eng_name, f, waits)
            sem.n += 16

        def op(eng_name, fn, sem=None, waits=(), tag=""):
            _sem = sem
            if _SERIAL or eng_name in _SERIAL2 or (tag and tag in _SERIAL3):
                waits = tuple(waits) + (pe, act, vec, ws, ld, gp, st)
            def f(e, _fn=fn):
                r = _fn(e)
                if _sem is not None:
                    r.then_inc(_sem.h, 1)
            b.step(eng_name, f, waits)
            if _sem is not None:
                _sem.n += 1

        def dbg_dump(name, src, waits):
            if name in dbg:
                dma("gpsimd", dbg[name], src, gp, waits)

        dma("sync", cp_sb2[:], cpool, ld)
        dma("sync", identb[:], ident_in, ld)
        dma("sync", ones1[:], ones1_in, ld)
        dma("sync", msel[:], msel_in, ld)
        dma("sync", ksel[:], ksel_in, ld)
        dma("sync", alog_f[:], alog_in, ld)
        dma("sync", bd_f[:], bd_in, ld)
        dma("sync", bi_f[:], bi_in, ld)
        dma("sync", bo_f[:], bo_in, ld)
        dma("sync", bs_f[:], bs_in, ld)
        dma("gpsimd", brow[:], borow_in, gp)
        dma("gpsimd", rowf[:], bsrow_in, gp)

        dma("gpsimd", wdwi[:, 0, :, :], wd_in[0].rearrange("(a p) n -> p a n", p=128), gp)
        dma("gpsimd", wdwi[:, 1, :, :], wi_in[0].rearrange("(a p) n -> p a n", p=128), gp)
        for l_ in range(NL):
            dma("gpsimd", wo_sb[:, :, l_, :], wo_in[l_].rearrange("(c p) d -> p c d", p=128), gp)
            dma("gpsimd", ws_sb[:, :, l_, :], ws_in[l_].rearrange("(c p) d -> p c d", p=128), gp)
        dma("gpsimd", x_bf[:], x_in.rearrange("(c p) d -> p c d", p=128), gp)

        op("vector", lambda e: e.tensor_tensor(out=brow[:], in0=brow[:], in1=rowf[:], op=Alu.add), vec, (gp,))
        op("scalar", lambda e: e.activation(out=sa_t[:], in_=alog_f[:], func=AF.Exp), act, (ld,))
        op("scalar", lambda e: e.activation(out=sa_t[:], in_=sa_t[:], func=AF.Ln, bias=1.0), act, (act,))
        op("vector", lambda e: e.tensor_single_scalar(out=nsa[:], in_=sa_t[:], scalar=-1.0, op=Alu.mult), vec, (act,))
        op("vector", lambda e: e.memset(zeros_b[:], 0.0), vec)

        for tc in range(TC):
            def mk(tc=tc):
                def f(e):
                    last = None
                    for dch in range(DCH):
                        last = e.transpose(psT[:, dch * 128:(dch + 1) * 128],
                                           x_bf[:, tc, dch * 128:(dch + 1) * 128], identb[:])
                    return last
                return f
            op("tensor", mk(), pe, (gp, ld) if tc == 0 else (act,))
            op("scalar", (lambda tc=tc: (lambda e: e.activation(
                out=x_fm[:, :, tc * 128:(tc + 1) * 128],
                in_=psT[:].rearrange("p (a t) -> p a t", t=128), func=AF.Identity)))(), act, (pe,))

        m_xfm_ready = act.mark()

        def emit_quant(src, dst, C, bufs, waits=(), sink=None):
            au, k1, k2, t3, sg = bufs
            if sink is None:
                OP = op
            else:
                def OP(eng, fn, sem=None, w=(), tag=""):
                    sink.append(lambda: op(eng, fn, sem, w, tag))
            nb = C // 16
            s3 = src.rearrange("p (n b) -> p n b", b=16)
            OP("vector", lambda e: e.tensor_reduce(out=q_bm[:, 0:nb].unsqueeze(2), in_=s3,
                                                   axis=AX, op=Alu.max, apply_absolute_value=True),
               vec, tuple(waits) + (vec,))
            OP("vector", lambda e: e.reciprocal(out=q_rr[:, 0:nb], in_=q_bm[:, 0:nb]), vec, (vec,))
            OP("vector", lambda e: e.tensor_scalar(out=q_rr[:, 0:nb], in0=q_rr[:, 0:nb], scalar1=6.0,
                                                   scalar2=1e12, op0=Alu.mult, op1=Alu.min), vec, (vec,))
            OP("vector", lambda e: e.tensor_scalar(out=q_sh[:, 0:nb], in0=q_bm[:, 0:nb], scalar1=1.0 / 6.0,
                                                   scalar2=1e-12, op0=Alu.mult, op1=Alu.max), vec, (vec,))
            rrb = q_rr[:, 0:nb].unsqueeze(2).broadcast_to([128, nb, 16])
            shb = q_sh[:, 0:nb].unsqueeze(2).broadcast_to([128, nb, 16])
            OP("vector", (lambda au=au: (lambda e: e.tensor_tensor(
                out=au[:, 0:C].rearrange("p (n b) -> p n b", b=16),
                in0=s3, in1=rrb, op=Alu.mult)))(), vec, (vec, act))
            m_srcfree_v = vec.mark() if sink is None else None
            OP("scalar", (lambda au=au: (lambda e: e.activation(out=au[:, 0:C], in_=au[:, 0:C], func=AF.Abs)))(), act, (vec,))
            OP("scalar", (lambda sg=sg: (lambda e: e.activation(out=sg[:, 0:C], in_=src, func=AF.Sign)))(),
               act, tuple(waits) + (vec,))
            m_srcfree_a = act.mark() if sink is None else None
            OP("vector", (lambda au=au, k1=k1: (lambda e: e.tensor_single_scalar(
                out=k1[:, 0:C], in_=au[:, 0:C], scalar=QTHR[0], op=Alu.is_gt)))(), vec, (act, pe))
            ks = [k1, k2]
            for i, t in enumerate(QTHR[1:]):
                sk, dk = ks[i % 2], ks[(i + 1) % 2]
                OP("vector", (lambda t=t, sk=sk, dk=dk, au=au: (lambda e: e.scalar_tensor_tensor(
                    out=dk[:, 0:C], in0=au[:, 0:C], scalar=float(t), in1=sk[:, 0:C],
                    op0=Alu.is_gt, op1=Alu.add)))(), vec, (vec,))
            kf = ks[len(QTHR[1:]) % 2]
            OP("vector", (lambda kf=kf, k2=k2: (lambda e: e.tensor_single_scalar(
                out=k2[:, 0:C], in_=kf[:, 0:C], scalar=0.5, op=Alu.mult)))(), vec, (vec,))
            OP("vector", (lambda kf=kf, t3=t3: (lambda e: e.tensor_single_scalar(
                out=t3[:, 0:C], in_=kf[:, 0:C], scalar=-2.0, op=Alu.add)))(), vec, (vec,))
            OP("vector", (lambda k2=k2, t3=t3: (lambda e: e.tensor_tensor(
                out=k2[:, 0:C], in0=k2[:, 0:C], in1=t3[:, 0:C], op=Alu.max)))(), vec, (vec,))
            OP("vector", (lambda kf=kf, t3=t3: (lambda e: e.tensor_scalar(
                out=t3[:, 0:C], in0=kf[:, 0:C], scalar1=2.0, scalar2=-8.0,
                op0=Alu.mult, op1=Alu.add)))(), vec, (vec,))
            OP("vector", (lambda k2=k2, t3=t3: (lambda e: e.tensor_tensor(
                out=k2[:, 0:C], in0=k2[:, 0:C], in1=t3[:, 0:C], op=Alu.max)))(), vec, (vec,))
            OP("vector", (lambda sg=sg, t3=t3: (lambda e: e.tensor_tensor(
                out=t3[:, 0:C].rearrange("p (n b) -> p n b", b=16),
                in0=sg[:, 0:C].rearrange("p (n b) -> p n b", b=16),
                in1=shb, op=Alu.mult)))(), vec, (vec, act))
            OP("vector", (lambda k2=k2, t3=t3: (lambda e: e.tensor_tensor(
                out=dst, in0=k2[:, 0:C], in1=t3[:, 0:C], op=Alu.mult)))(), vec, (vec, pe))
            return (m_srcfree_v, m_srcfree_a)

        def emit_quant_A(src, C, au, sg, bmo, waits=()):
            nb = C // 16
            s3 = src.rearrange("p (n b) -> p n b", b=16)
            op("vector", (lambda s3=s3, nb=nb, bmo=bmo: (lambda e: e.tensor_reduce(
                out=q_bm[:, bmo:bmo + nb].unsqueeze(2), in_=s3,
                axis=AX, op=Alu.max, apply_absolute_value=True)))(), vec, tuple(waits) + (vec,))
            op("vector", (lambda nb=nb, bmo=bmo: (lambda e: e.reciprocal(
                out=q_rr[:, bmo:bmo + nb], in_=q_bm[:, bmo:bmo + nb])))(), vec, (vec,))
            op("vector", (lambda nb=nb, bmo=bmo: (lambda e: e.tensor_scalar(
                out=q_rr[:, bmo:bmo + nb], in0=q_rr[:, bmo:bmo + nb], scalar1=6.0,
                scalar2=1e12, op0=Alu.mult, op1=Alu.min)))(), vec, (vec,))
            op("vector", (lambda nb=nb, bmo=bmo: (lambda e: e.tensor_scalar(
                out=q_sh[:, bmo:bmo + nb], in0=q_bm[:, bmo:bmo + nb], scalar1=1.0 / 6.0,
                scalar2=1e-12, op0=Alu.mult, op1=Alu.max)))(), vec, (vec,))
            rrb = q_rr[:, bmo:bmo + nb].unsqueeze(2).broadcast_to([128, nb, 16])
            op("vector", (lambda au=au, s3=s3, rrb=rrb, C=C: (lambda e: e.tensor_tensor(
                out=au[:, 0:C].rearrange("p (n b) -> p n b", b=16),
                in0=s3, in1=rrb, op=Alu.mult)))(), vec, (vec, act))
            op("scalar", (lambda au=au, C=C: (lambda e: e.activation(
                out=au[:, 0:C], in_=au[:, 0:C], func=AF.Abs)))(), act, (vec,))
            op("scalar", (lambda sg=sg, src=src, C=C: (lambda e: e.activation(
                out=sg[:, 0:C], in_=src, func=AF.Sign)))(), act, tuple(waits) + (vec,))
            return act.mark()

        def emit_quant_B(dst, C, au, sg, bmo, m_absg):
            nb = C // 16
            shb = q_sh[:, bmo:bmo + nb].unsqueeze(2).broadcast_to([128, nb, 16])
            k1, k2, t3 = n_k1, n_k2, n_t3
            op("vector", (lambda au=au, k1=k1, C=C: (lambda e: e.tensor_single_scalar(
                out=k1[:, 0:C], in_=au[:, 0:C], scalar=QTHR[0], op=Alu.is_gt)))(), vec, (m_absg,))
            ks = [k1, k2]
            for i, t in enumerate(QTHR[1:]):
                sk, dk = ks[i % 2], ks[(i + 1) % 2]
                op("vector", (lambda t=t, sk=sk, dk=dk, au=au, C=C: (lambda e: e.scalar_tensor_tensor(
                    out=dk[:, 0:C], in0=au[:, 0:C], scalar=float(t), in1=sk[:, 0:C],
                    op0=Alu.is_gt, op1=Alu.add)))(), vec, (vec,))
            kf = ks[len(QTHR[1:]) % 2]
            op("vector", (lambda kf=kf, k2=k2, C=C: (lambda e: e.tensor_single_scalar(
                out=k2[:, 0:C], in_=kf[:, 0:C], scalar=0.5, op=Alu.mult)))(), vec, (vec,))
            op("vector", (lambda kf=kf, t3=t3, C=C: (lambda e: e.tensor_single_scalar(
                out=t3[:, 0:C], in_=kf[:, 0:C], scalar=-2.0, op=Alu.add)))(), vec, (vec,))
            op("vector", (lambda k2=k2, t3=t3, C=C: (lambda e: e.tensor_tensor(
                out=k2[:, 0:C], in0=k2[:, 0:C], in1=t3[:, 0:C], op=Alu.max)))(), vec, (vec,))
            op("vector", (lambda kf=kf, t3=t3, C=C: (lambda e: e.tensor_scalar(
                out=t3[:, 0:C], in0=kf[:, 0:C], scalar1=2.0, scalar2=-8.0,
                op0=Alu.mult, op1=Alu.add)))(), vec, (vec,))
            op("vector", (lambda k2=k2, t3=t3, C=C: (lambda e: e.tensor_tensor(
                out=k2[:, 0:C], in0=k2[:, 0:C], in1=t3[:, 0:C], op=Alu.max)))(), vec, (vec,))
            op("vector", (lambda sg=sg, t3=t3, shb=shb, C=C: (lambda e: e.tensor_tensor(
                out=t3[:, 0:C].rearrange("p (n b) -> p n b", b=16),
                in0=sg[:, 0:C].rearrange("p (n b) -> p n b", b=16),
                in1=shb, op=Alu.mult)))(), vec, (vec, act))
            op("vector", (lambda k2=k2, t3=t3, dst=dst, C=C: (lambda e: e.tensor_tensor(
                out=dst, in0=k2[:, 0:C], in1=t3[:, 0:C], op=Alu.mult)))(), vec, (vec, pe))
            return vec.mark()

        NBUF = (n_au[:], n_k1[:], n_k2[:], n_t3[:], n_sg[:])
        WBUF = (w_au[:], w_k1[:], w_k2[:], w_t3[:], w_sg[:])
        DBUF = (d_au[:], d_k1[:], d_k2[:], d_t3[:], d_sg[:])

        wq_jobs = []
        for s in range(4):
            wq_jobs.append(("wg", s, D, wg_sl[s * 128:(s + 1) * 128, :]))
        for s in range(4):
            wq_jobs.append(("wu", s, D, wu_sl[s * 128:(s + 1) * 128, :]))
        wq_jobs.append(("wd", 0, 1024, wdn_sl[:, 0:1024]))
        wq_jobs.append(("wd", 1, 1024, wdn_sl[:, 1024:2048]))
        wq_jobs.append(("wd", 2, I - 2048, wdn_sl[:, 2048:I]))
        if _STAGE < 2 or _NOW:
            wq_jobs = []

        def wjob_thunks(job):
            kind, s, C, src_ap = job
            th = []
            th.append(lambda: dma("sync", hrow[:, 0:C], src_ap, s_wh, (vec,)))
            emit_quant(hrow[:, 0:C], d_k1[:, 0:C], C, DBUF, waits=(s_wh,), sink=th)
            ntile = C // 128
            def mkT(ntile=ntile):
                def f(e):
                    last = None
                    for j in range(ntile):
                        last = e.transpose(psT[:, j * 128:(j + 1) * 128],
                                           d_k1[:, j * 128:(j + 1) * 128], identb[:])
                    return last
                return f
            th.append(lambda: op("tensor", mkT(), pe, (vec, act), tag="wT"))
            th.append(lambda: op("scalar", (lambda ntile=ntile: (lambda e: e.activation(
                out=wTst[:, 0:ntile, :],
                in_=psT[:, 0:ntile * 128].rearrange("p (a t) -> p a t", t=128), func=AF.Identity)))(), act, (pe, ws)))
            if kind == "wg":
                dst_ap = gin_wg.rearrange("p (a s t) -> p a s t", a=DCH, s=4)[:, :, s, :]
            elif kind == "wu":
                dst_ap = gin_wu.rearrange("p (a s t) -> p a s t", a=DCH, s=4)[:, :, s, :]
            else:
                dst_ap = gin_wd[s * 1024:s * 1024 + C, :].rearrange("(a p) j -> p a j", p=128)
            th.append(lambda: dma("sync", dst_ap, wTst[:, 0:ntile, :], ws, (act,)))
            return th

        _wq_thunks = []
        for _j in wq_jobs:
            _wq_thunks.extend(wjob_thunks(_j))
        if not _ILW:
            for _t in _wq_thunks:
                _t()
            _wq_thunks = []

        def emit_wjob_maybe(n=1):
            for _ in range(n):
                if _wq_thunks:
                    _wq_thunks.pop(0)()

        def emit_wgather():
            for gi, go in (((gin_wg, gout_wg), (gin_wu, gout_wu), (gin_wd, gout_wd)) if _STAGE >= 2 else ()):
                def mkcc(gi=gi, go=go):
                    def f(e):
                        return e.collective_compute("AllGather", Alu.bypass,
                                                    replica_groups=[list(range(NC_))],
                                                    ins=[gi], outs=[go])
                    return f
                op("gpsimd", mkcc(), cc, (ws,))
            for cpr in (range(NC_) if _STAGE >= 2 else ()):
                dma("gpsimd", wdq2[:, cpr * 128:(cpr + 1) * 128],
                    gout_wd[cpr * I:(cpr + 1) * I, :], gp, (cc,))

        m_rhs = m_xfm_ready
        m_psA = None
        m_aDl = None
        m_aDc = None
        m_psB = None
        m_wdwi = gp.mark()
        for l in range(NL if _STAGE >= 3 else 0):
            rhs = x_fm if l == 0 else h1_fm
            if l == 1:
                dma("gpsimd", wdwi[:, 0, :, :], wd_in[1].rearrange("(a p) n -> p a n", p=128), s_wdwi, (pe,))
                dma("gpsimd", wdwi[:, 1, :, :], wi_in[1].rearrange("(a p) n -> p a n", p=128), s_wdwi, (pe,))
                m_wdwi = s_wdwi.mark()

            for nch in range(NCH):
                def mk_d(l=l, nch=nch, rhs=rhs):
                    def f(e):
                        last = None
                        for tt in range(4):
                            for kk in range(DCH):
                                last = e.matmul(psA[:, tt * 512:(tt + 1) * 512],
                                                wdwi[:, 0, kk, nch * 128:(nch + 1) * 128],
                                                rhs[:, kk, tt * 512:(tt + 1) * 512],
                                                start=(kk == 0), stop=(kk == DCH - 1))
                        return last
                    return f
                op("tensor", mk_d(), pe,
                   (m_rhs, m_wdwi) + ((m_psA,) if m_psA is not None else ()), tag="dmm")
                m_dmm = pe.mark()
                emit_wjob_maybe(2)
                op("scalar", (lambda l=l, nch=nch: (lambda e: e.activation(
                    out=a_tmp[:], in_=psA[:], func=AF.Exp,
                    bias=bd_f[:, nch, l].unsqueeze(1))))(), act, (m_dmm,))
                m_psA = act.mark()
                emit_wjob_maybe(2)
                op("scalar", lambda e: e.activation(out=a_dl[:], in_=a_tmp[:], func=AF.Ln, bias=1.0), act,
                   ((m_aDl,) if m_aDl is not None else ()))
                m_ln = act.mark()
                emit_wjob_maybe(2)
                op("scalar", (lambda l=l, nch=nch: (lambda e: e.activation(
                    out=a_dc[:], in_=a_dl[:], func=AF.Exp,
                    scale=nsa[:, nch, l].unsqueeze(1))))(), act,
                   ((m_aDc,) if m_aDc is not None else ()))
                m_decay = act.mark()
                emit_wjob_maybe(2)
                for half in range(2):
                    def mk_u(l=l, nch=nch, half=half, rhs=rhs):
                        def f(e):
                            last = None
                            for tt in range(2):
                                t0 = half * 2 + tt
                                for kk in range(DCH):
                                    last = e.matmul(psB[:, tt * 512:(tt + 1) * 512],
                                                    wdwi[:, 1, kk, nch * 128:(nch + 1) * 128],
                                                    rhs[:, kk, t0 * 512:(t0 + 1) * 512],
                                                    start=(kk == 0), stop=(kk == DCH - 1))
                            return last
                        return f
                    op("tensor", mk_u(), pe,
                       (m_rhs, m_wdwi) + ((m_psB,) if m_psB is not None else ()), tag="umm")
                    emit_wjob_maybe(2)
                    m_umm = pe.mark()
                    op("vector", (lambda l=l, nch=nch, half=half: (lambda e: e.scalar_tensor_tensor(
                        out=a_dr[:, half * 1024:(half + 1) * 1024],
                        in0=psB[:, 0:1024], scalar=bi_f[:, nch, l].unsqueeze(1),
                        in1=a_dl[:, half * 1024:(half + 1) * 1024],
                        op0=Alu.add, op1=Alu.mult)))(), vec, (m_umm, m_ln))
                    m_psB = vec.mark()
                    emit_wjob_maybe(2)
                m_aDl = m_psB
                nseg = T // SCAN_SEG
                for seg in range(nseg):
                    s0, s1 = seg * SCAN_SEG, (seg + 1) * SCAN_SEG
                    ini_h = 0.0 if seg == 0 else scan_carry[:, 0:1]
                    ini_c = 1.0 if seg == 0 else scan_carry[:, 1:2]
                    op("vector", (lambda s0=s0, s1=s1, ini=ini_h, nch=nch: (lambda e: e.tensor_tensor_scan(
                        out=hs_sb[:, nch, s0:s1], data0=a_dc[:, s0:s1], data1=a_dr[:, s0:s1],
                        initial=ini, op0=Alu.mult, op1=Alu.add)))(), vec, (m_decay,))
                    op("vector", (lambda s0=s0, s1=s1, ini=ini_c, nch=nch: (lambda e: e.tensor_tensor_scan(
                        out=cp_sb[:, nch, s0:s1], data0=a_dc[:, s0:s1], data1=zeros_b[:, s0:s1],
                        initial=ini, op0=Alu.mult, op1=Alu.add)))(), vec, ())
                    emit_wjob_maybe(2)
                    if seg < nseg - 1:
                        op("vector", (lambda s1=s1, nch=nch: (lambda e: e.tensor_copy(
                            out=scan_carry[:, 0:1], in_=hs_sb[:, nch, s1 - 1:s1])))(), vec, (vec,))
                        op("vector", (lambda s1=s1, nch=nch: (lambda e: e.tensor_copy(
                            out=scan_carry[:, 1:2], in_=cp_sb[:, nch, s1 - 1:s1])))(), vec, (vec,))
                m_aDc = vec.mark()
                op("vector", (lambda nch=nch: (lambda e: e.tensor_copy(
                    out=pk_sb[:, nch:nch + 1], in_=cp_sb[:, nch, T - 1:T])))(), vec, (vec,))
                op("vector", (lambda nch=nch: (lambda e: e.tensor_copy(
                    out=pk_sb[:, 2 + nch:3 + nch], in_=hs_sb[:, nch, T - 1:T])))(), vec, (vec,))
                emit_wjob_maybe(2)

            if l == 1:
                dbg_dump("hs1", hs_sb[:].rearrange("p a t -> p (a t)"), (vec,))
            else:
                dbg_dump("hs0", hs_sb[:].rearrange("p a t -> p (a t)"), (vec,))
                dbg_dump("cp0", cp_sb[:].rearrange("p a t -> p (a t)"), (vec,))

            dma("sync", gin_s[l], pk_sb[:], ws, (vec,))
            def mkccs(l=l):
                def f(e):
                    return e.collective_compute("AllGather", Alu.bypass,
                                                replica_groups=[list(range(NC_))],
                                                ins=[gin_s[l]], outs=[gout_s[l]])
                return f
            op("gpsimd", mkccs(), cc, (ws,))
            dma("sync", summ[:], gout_s[l].rearrange("(c p) f -> p c f", p=128), s_summ, (cc, vec))
            m_summ = s_summ.mark()

            for nch in range(NCH):
                op("vector", (lambda nch=nch: (lambda e: e.memset(sj[:, nch, 0:1], 0.0)))(), vec, (m_summ,))
                for j in range(4):
                    op("vector", (lambda j=j, nch=nch: (lambda e: e.tensor_tensor(
                        out=jsel[:], in0=summ[:, :, nch], in1=msel[:, j, :], op=Alu.mult)))(), vec, (vec,))
                    op("vector", lambda e: e.tensor_reduce(out=st1[:, 0:1], in_=jsel[:], axis=AX, op=Alu.add), vec, (vec,))
                    op("vector", (lambda j=j, nch=nch: (lambda e: e.tensor_tensor(
                        out=jsel[:], in0=summ[:, :, 2 + nch], in1=msel[:, j, :], op=Alu.mult)))(), vec, (vec,))
                    op("vector", lambda e: e.tensor_reduce(out=st1[:, 1:2], in_=jsel[:], axis=AX, op=Alu.add), vec, (vec,))
                    op("vector", (lambda j=j, nch=nch: (lambda e: e.tensor_tensor(
                        out=st1[:, 2:3], in0=st1[:, 0:1], in1=sj[:, nch, j:j + 1], op=Alu.mult)))(), vec, (vec,))
                    op("vector", (lambda j=j, nch=nch: (lambda e: e.tensor_tensor(
                        out=sj[:, nch, j + 1:j + 2], in0=st1[:, 2:3], in1=st1[:, 1:2], op=Alu.add)))(), vec, (vec,))
                    emit_wjob_maybe(2)
                op("vector", (lambda nch=nch: (lambda e: e.tensor_tensor(
                    out=s4sel[:], in0=sj[:, nch, 0:4], in1=ksel[:], op=Alu.mult)))(), vec, (vec,))
                op("vector", (lambda nch=nch: (lambda e: e.tensor_reduce(
                    out=hin[:, nch:nch + 1], in_=s4sel[:], axis=AX, op=Alu.add)))(), vec, (vec,))
                op("vector", (lambda nch=nch: (lambda e: e.tensor_copy(
                    out=st4b[:, nch:nch + 1], in_=sj[:, nch, 4:5])))(), vec, (vec,))
                op("vector", (lambda nch=nch: (lambda e: e.scalar_tensor_tensor(
                    out=hsp[:, nch, :], in0=cp_sb[:, nch, :], scalar=hin[:, nch:nch + 1],
                    in1=hs_sb[:, nch, :], op0=Alu.mult, op1=Alu.add)))(), vec, (vec,))
            m_hsp = vec.mark()
            emit_wjob_maybe(4)

            if l == 1:
                dbg_dump("hsp1", hsp[:].rearrange("p a t -> p (a t)"), (vec,))
            else:
                dbg_dump("hsp0", hsp[:].rearrange("p a t -> p (a t)"), (vec,))
                dbg_dump("summ0", summ[:].rearrange("p c f -> p (c f)"), (vec,))
                dbg_dump("pk0", pk_sb[:], (vec,))
                if "gin0" in dbg:
                    dma("gpsimd", dbg["gin0"], gin_s[0], gp, (cc,))
                if "gout0" in dbg:
                    dma("gpsimd", dbg["gout0"], gout_s[0], gp, (cc,))

            def mk_cs(l=l):
                def f(e):
                    last = None
                    for dch in range(DCH):
                        for nch in range(NCH):
                            last = e.matmul(psB[:, 1024 + dch:1024 + dch + 1],
                                            ws_sb[:, nch, l, dch * 128:(dch + 1) * 128],
                                            st4b[:, nch:nch + 1],
                                            start=(nch == 0), stop=(nch == NCH - 1))
                    return last
                return f
            op("tensor", mk_cs(), pe, (m_hsp,), tag="cs")
            m_csm = pe.mark()
            op("vector", (lambda l=l: (lambda e: e.tensor_tensor(
                out=bvec[:], in0=psB[:, 1024:1024 + DCH], in1=bs_f[:, :, l], op=Alu.add)))(), vec, (m_csm,))
            if l == 0:
                op("vector", lambda e: e.tensor_tensor(out=bvec0[:], in0=bvec[:], in1=bo_f[:, :, 0], op=Alu.add), vec, (vec,))
                m_bv0 = vec.mark()
                for dch in range(DCH):
                    def mk_y0(dch=dch):
                        def f(e):
                            last = None
                            for tt in range(4):
                                for nch in range(NCH):
                                    last = e.matmul(psA[:, tt * 512:(tt + 1) * 512],
                                                    wo_sb[:, nch, 0, dch * 128:(dch + 1) * 128],
                                                    hsp[:, nch, tt * 512:(tt + 1) * 512],
                                                    start=(nch == 0), stop=(nch == NCH - 1))
                            return last
                        return f
                    op("tensor", mk_y0(), pe, (m_hsp,) if dch == 0 else (m_psA,), tag="y0")
                    m_y0 = pe.mark()
                    emit_wjob_maybe(3)
                    op("scalar", (lambda dch=dch: (lambda e: e.activation(
                        out=h1_fm[:, dch, :], in_=psA[:], func=AF.Identity,
                        bias=bvec0[:, dch].unsqueeze(1))))(), act, (m_y0, m_bv0))
                    m_psA = act.mark()
                m_rhs = m_psA
            else:
                def mk_csr(e):
                    last = None
                    for dh in range(2):
                        for nch in range(NCH):
                            last = e.matmul(psB[0:1, dh * 512:(dh + 1) * 512],
                                            st4b[:, nch:nch + 1],
                                            ws_sb[:, nch, 1, dh * 512:(dh + 1) * 512],
                                            start=(nch == 0), stop=(nch == NCH - 1))
                    return last
                op("tensor", mk_csr, pe, (m_hsp, m_psB), tag="csr")
                m_csr = pe.mark()
                op("scalar", lambda e: e.activation(out=rowf[:], in_=psB[0:1, 0:1024], func=AF.Identity), act, (m_csr,))
                op("vector", lambda e: e.tensor_tensor(out=r1row[:], in0=rowf[:], in1=brow[:], op=Alu.add), vec, (act,))

        emit_wjob_maybe(len(_wq_thunks))
        emit_wgather()

        if _STAGE >= 3 and "h1fm" in dbg:
            dbg_dump("h1fm", h1_fm[:].rearrange("p a t -> p (a t)"), (act,))

        ring_pre1 = [None, None]
        m_x1act = None
        m_cast = None
        m_absg = [None, None]
        m_qB = [None, None]
        m_xqev = None
        for tcx in range(TC + 2 if _STAGE >= 4 else 0):
          if tcx < TC:
            tc = tcx
            pbuf = tc % 2
            o0 = 0 if pbuf == 0 else 1024
            slot_w = [m for m in (ring_pre1[pbuf],) if m is not None]
            dma("sync", xres[:, pbuf, :], x_in[tc * 128:(tc + 1) * 128, :], sxr_sl[pbuf],
                tuple(slot_w) + ((vec, pe) if tc == 0 else ()))
            def mk_y1(tc=tc, o0=o0):
                def f(e):
                    last = None
                    for dh in range(2):
                        for nch in range(NCH):
                            e.matmul(psA[:, o0 + dh * 512:o0 + (dh + 1) * 512],
                                     hsp[:, nch, tc * 128:(tc + 1) * 128],
                                     wo_sb[:, nch, 1, dh * 512:(dh + 1) * 512],
                                     start=(nch == 0), stop=False)
                        last = e.matmul(psA[:, o0 + dh * 512:o0 + (dh + 1) * 512],
                                        ones1[:], r1row[:, dh * 512:(dh + 1) * 512],
                                        start=False, stop=True)
                    return last
                return f
            op("tensor", mk_y1(), pe, tuple(slot_w) + ((vec,) if tc == 0 else ()), tag="y1")
            m_y1 = pe.mark()
            op("vector", (lambda o0=o0, pbuf=pbuf: (lambda e: e.tensor_tensor(
                out=pre1[:], in0=psA[:, o0:o0 + 1024], in1=xres[:, pbuf, :],
                op=Alu.add)))(), vec,
               (m_y1, sxr_sl[pbuf].mark()) + ((m_x1act,) if m_x1act is not None else ()))
            ring_pre1[pbuf] = vec.mark()
            op("vector", lambda e: e.tensor_reduce(out=st1[:, 0:1], in_=pre1[:], axis=AX, op=Alu.add), vec, (vec,))
            op("scalar", lambda e: e.activation(out=sqscr[:], in_=pre1[:], func=AF.Square,
                                                accum_out=st1[:, 1:2]), act, (vec,))
            op("vector", lambda e: e.tensor_single_scalar(out=st1[:, 2:3], in_=st1[:, 0:1], scalar=1.0 / D, op=Alu.mult), vec, (vec, act))
            op("vector", lambda e: e.tensor_tensor(out=st1[:, 3:4], in0=st1[:, 2:3], in1=st1[:, 2:3], op=Alu.mult), vec, (vec,))
            op("vector", lambda e: e.scalar_tensor_tensor(out=st1[:, 4:5], in0=st1[:, 1:2], scalar=1.0 / D,
                                                          in1=st1[:, 3:4], op0=Alu.mult, op1=Alu.subtract), vec, (vec,))
            op("scalar", lambda e: e.activation(out=st1[:, 5:6], in_=st1[:, 4:5], func=AF.Ln, bias=EPS), act, (vec,))
            op("scalar", lambda e: e.activation(out=st1[:, 5:6], in_=st1[:, 5:6], func=AF.Exp, scale=-0.5), act, (act,))
            op("vector", lambda e: e.tensor_tensor(out=st1[:, 6:7], in0=st1[:, 2:3], in1=st1[:, 5:6], op=Alu.mult), vec, (act,))
            op("vector", lambda e: e.tensor_single_scalar(out=st1[:, 6:7], in_=st1[:, 6:7], scalar=-1.0, op=Alu.mult), vec, (vec,))
            op("scalar", lambda e: e.activation(out=x1f[:], in_=pre1[:], func=AF.Identity,
                                                scale=st1[:, 5:6], bias=st1[:, 6:7]), act,
               ((vec,) + ((m_cast,) if m_cast is not None else ())))
            m_x1act = act.mark()
            op("scalar", lambda e: e.activation(out=x1b[:], in_=x1f[:], func=AF.Identity), act, (s_x1b,))
            m_cast = act.mark()
            dma("sync", x1_d[tc * 128:(tc + 1) * 128, :], x1b[:], s_x1b, (m_cast,))
            au_p = n_au if pbuf == 0 else n_au2
            sg_p = n_sg if pbuf == 0 else n_sg2
            m_absg[pbuf] = emit_quant_A(x1f[:], D, au_p, sg_p, pbuf * 64, waits=(act,))
          if 1 <= tcx < TC + 1:
            tc = tcx - 1
            pbuf = tc % 2
            au_p = n_au if pbuf == 0 else n_au2
            sg_p = n_sg if pbuf == 0 else n_sg2
            dst_p = n_dst if pbuf == 0 else n_dst2
            m_qB[pbuf] = emit_quant_B(dst_p[:], D, au_p, sg_p, pbuf * 64, m_absg[pbuf])
          if tcx >= 2:
            tc = tcx - 2
            pbuf = tc % 2
            dst_p = n_dst if pbuf == 0 else n_dst2
            def mk_tq(tc=tc, dst_p=dst_p):
                def f(e):
                    last = None
                    for dch in range(DCH):
                        last = e.transpose(psT[:, dch * 128:(dch + 1) * 128],
                                           dst_p[:, dch * 128:(dch + 1) * 128], identb[:])
                    return last
                return f
            op("tensor", mk_tq(), pe, (m_qB[pbuf],) + ((m_xqev,) if m_xqev is not None else ()), tag="tq")
            m_tq = pe.mark()
            op("scalar", (lambda tc=tc: (lambda e: e.activation(
                out=xq_fm[:, :, tc * 128:(tc + 1) * 128],
                in_=psT[:].rearrange("p (a t) -> p a t", t=128), func=AF.Identity)))(), act, (m_tq,))
            m_xqev = act.mark()

        if _STAGE >= 4:
            dbg_dump("x1", x1_d, (gp, act, st))
            dbg_dump("xq", xq_fm[:].rearrange("p a t -> p (a t)"), (act,))

        ITS = [(0, 512), (512, 512), (1024, 512), (1536, 512), (2048, 512), (2560, 256)]
        ring_slotA = [None, None]
        ring_wld = [None, None]
        ring_dn = [None] * 3
        m_hrow_free = None
        m_pre2 = None
        ring_x1r = [None, None, None]
        m_evac = [None, None, None]
        m_prevq = [None, None]
        m_ssq = [None, None]
        m_pre2s_l = [None, None]
        m_out_done = [None, None]
        for tcx in range(TC + 2 if _STAGE >= 5 else 0):
            if _STAGE >= 6 and 2 <= tcx:
                tc4 = tcx - 2
                p4 = tc4 % 2
                pre2s_r = pre2s if p4 == 0 else pre2s2
                op("vector", (lambda tc4=tc4: (lambda e: e.tensor_single_scalar(
                    out=st1[:, 2:3], in_=sums2[:, tc4:tc4 + 1], scalar=1.0 / D, op=Alu.mult)))(), vec, (vec,))
                op("vector", lambda e: e.tensor_tensor(out=st1[:, 3:4], in0=st1[:, 2:3], in1=st1[:, 2:3], op=Alu.mult), vec, (vec,))
                op("vector", (lambda tc4=tc4: (lambda e: e.scalar_tensor_tensor(
                    out=st1[:, 4:5], in0=ssq2[:, tc4:tc4 + 1], scalar=1.0 / D,
                    in1=st1[:, 3:4], op0=Alu.mult, op1=Alu.subtract)))(), vec, (vec, m_ssq[p4]))
                op("scalar", lambda e: e.activation(out=st1[:, 5:6], in_=st1[:, 4:5], func=AF.Ln, bias=EPS), act, (vec,))
                op("scalar", lambda e: e.activation(out=st1[:, 5:6], in_=st1[:, 5:6], func=AF.Exp, scale=-0.5), act, (act,))
                op("vector", lambda e: e.tensor_tensor(out=st1[:, 6:7], in0=st1[:, 2:3], in1=st1[:, 5:6], op=Alu.mult), vec, (act,))
                op("vector", lambda e: e.tensor_single_scalar(out=st1[:, 6:7], in_=st1[:, 6:7], scalar=-1.0, op=Alu.mult), vec, (vec,))
                m_negb = vec.mark()
                op("scalar", (lambda pre2s_r=pre2s_r: (lambda e: e.activation(
                    out=d_au[:, 0:D], in_=pre2s_r[:], func=AF.Identity,
                    scale=st1[:, 5:6], bias=st1[:, 6:7])))(), act,
                   (m_negb, sp2_sl[0].mark(), m_pre2s_l[p4]))
                m_outact = act.mark()
                dma("sync", out[tc4 * 128:(tc4 + 1) * 128, :], d_au[:, 0:D], sp2_sl[0], (m_outact,))
                m_out_done[p4] = m_outact
            if tcx < TC:
                tc = tcx
                x1r_w = (ring_x1r[tc % 3],) if ring_x1r[tc % 3] is not None else ()
                dma("sync", x1r_s(tc % 3), x1_d[tc * 128:(tc + 1) * 128, :], sx1_sl[tc % 3],
                    x1r_w + ((vec, st, gp, s_x1b) if tc < 2 else ()))
                for iti, (i0, ilen) in enumerate(ITS):
                    wbuf = iti % 2
                    go = 0 if (iti % 2 == 0) else 1024
                    wld_w = (ring_wld[wbuf],) if ring_wld[wbuf] is not None else ()
                    dma("sync", wgst[:, wbuf, :, :],
                        gout_wg[iti * 128:(iti + 1) * 128, :].rearrange("p (a j) -> p a j", a=DCH),
                        sg_sl[wbuf], wld_w + ((cc,) if tc == 0 else ()))
                    dma("sync", wust[:, wbuf, :, :],
                        gout_wu[iti * 128:(iti + 1) * 128, :].rearrange("p (a j) -> p a j", a=DCH),
                        su_sl[wbuf], wld_w + ((cc,) if tc == 0 else ()))
                    m_wld = (sg_sl[wbuf].mark(), su_sl[wbuf].mark())
                    def mk_g(tc=tc, go=go, wbuf=wbuf, ilen=ilen):
                        def f(e):
                            last = None
                            for kk in range(DCH):
                                e.matmul(psA[:, go:go + ilen],
                                         xq_fm[:, kk, tc * 128:(tc + 1) * 128],
                                         wgst[:, wbuf, kk, 0:ilen],
                                         start=(kk == 0), stop=(kk == DCH - 1))
                            for kk in range(DCH):
                                last = e.matmul(psA[:, go + 512:go + 512 + ilen],
                                                xq_fm[:, kk, tc * 128:(tc + 1) * 128],
                                                wust[:, wbuf, kk, 0:ilen],
                                                start=(kk == 0), stop=(kk == DCH - 1))
                            return last
                        return f
                    slotA_w = ring_slotA[iti % 2] or ()
                    _gx = {"v": (vec,), "a": (act,), "w": (ws, sg_sl[wbuf], su_sl[wbuf]), "": ()}[_GEXTRA]
                    op("tensor", mk_g(), pe, m_wld + tuple(slotA_w) + _gx +
                       (((vec, act) if tc == 0 and iti < 2 else ())), tag="g")
                    m_gate = pe.mark()
                    sq0 = (iti % 2) * 512
                    sq_w = ring_slotA[iti % 2][1:] if ring_slotA[iti % 2] else ()
                    op("scalar", (lambda go=go, ilen=ilen, sq0=sq0: (lambda e: e.activation(
                        out=sqscr[:, sq0:sq0 + ilen], in_=psA[:, go:go + ilen], func=AF.Silu)))(), act,
                       (m_gate,) + tuple(sq_w))
                    m_silu = act.mark()
                    hrow_w = ()
                    if iti == 0 and m_hrow_free is not None:
                        hrow_w = m_hrow_free
                    op("vector", (lambda go=go, ilen=ilen, i0=i0, sq0=sq0: (lambda e: e.tensor_tensor(
                        out=hrow[:, i0:i0 + ilen], in0=sqscr[:, sq0:sq0 + ilen],
                        in1=psA[:, go + 512:go + 512 + ilen], op=Alu.mult)))(), vec,
                       (m_silu,) + tuple(hrow_w))
                    ring_slotA[iti % 2] = (m_silu, vec.mark())
                    ring_wld[wbuf] = m_gate
            if 1 <= tcx < TC + 1:
                tc = tcx - 1
                qsrc = w_dst if tc % 2 == 0 else w_dst2
                for gi_, (grp, gn) in enumerate(((0, 8), (8, 8), (16, 6))):
                    def mk_th(grp=grp, gn=gn, qsrc=qsrc):
                        def f(e):
                            last = None
                            for j in range(gn):
                                last = e.transpose(psT[:, j * 128:(j + 1) * 128],
                                                   qsrc[:, (grp + j) * 128:(grp + j + 1) * 128], identb[:])
                            return last
                        return f
                    prev_evac = m_evac[(gi_ + 2) % 3]
                    op("tensor", mk_th(), pe, (m_prevq[tc % 2],) +
                       ((prev_evac,) if prev_evac is not None else ()) +
                       ((act,) if tc == 0 else ()), tag="th")
                    m_tp = pe.mark()
                    op("scalar", (lambda grp=grp, gn=gn: (lambda e: e.activation(
                        out=hq_fm[:, grp:grp + gn, :],
                        in_=psT[:, 0:gn * 128].rearrange("p (a t) -> p a t", t=128), func=AF.Identity)))(), act, (m_tp,))
                    m_evac[gi_] = act.mark()
            if tcx < TC:
                tc = tcx
                qdst = w_dst if tc % 2 == 0 else w_dst2
                m_hrow_free = emit_quant(hrow[:], qdst[:], I, WBUF,
                                         waits=(m_prevq[tc % 2],) if m_prevq[tc % 2] is not None else ())
                m_prevq[tc % 2] = vec.mark()
            if 1 <= tcx < TC + 1:
                tc = tcx - 1
                for ich in range(ICH):
                    ring = ich % 3
                    dn_w = (ring_dn[ring],) if ring_dn[ring] is not None else ()
                    dma("gpsimd", wdst[:, ring, :], wdq2[ich * 128:(ich + 1) * 128, :], sd_sl[ring],
                        dn_w + ((gp,) if tc == 0 else ()))
                    m_wdl = sd_sl[ring].mark()
                    def mk_dn(tc=tc, ich=ich, ring=ring):
                        def f(e):
                            last = None
                            for dh in range(2):
                                last = e.matmul(psB[:, dh * 512:(dh + 1) * 512],
                                                hq_fm[:, ich, :],
                                                wdst[:, ring, dh * 512:(dh + 1) * 512],
                                                start=(ich == 0), stop=(ich == ICH - 1))
                            return last
                        return f
                    extra = ()
                    if ich == 0:
                        extra = tuple(m for m in m_evac if m is not None)
                        if m_pre2 is not None:
                            extra = extra + (m_pre2,)
                    op("tensor", mk_dn(), pe, (m_wdl,) + extra, tag="dn")
                    ring_dn[ring] = pe.mark()
                m_dn = pe.mark()
                pre2s_p = pre2s if tc % 2 == 0 else pre2s2
                op("vector", (lambda tc=tc, pre2s_p=pre2s_p: (lambda e: e.tensor_tensor(
                    out=pre2s_p[:], in0=psB[:, 0:1024], in1=x1r_s(tc % 3),
                    op=Alu.add)))(), vec, (m_dn, sx1_sl[tc % 3].mark(), act) +
                   ((m_out_done[tc % 2],) if m_out_done[tc % 2] is not None else ()))
                m_pre2 = vec.mark()
                ring_x1r[tc % 3] = m_pre2
                op("vector", (lambda tc=tc, pre2s_p=pre2s_p: (lambda e: e.tensor_reduce(
                    out=sums2[:, tc:tc + 1], in_=pre2s_p[:], axis=AX, op=Alu.add)))(), vec, (vec,))
                op("scalar", (lambda tc=tc, pre2s_p=pre2s_p: (lambda e: e.activation(
                    out=sqscr[:], in_=pre2s_p[:], func=AF.Square,
                    accum_out=ssq2[:, tc:tc + 1])))(), act, (vec,))
                m_ssq[tc % 2] = act.mark()
                m_pre2s_l[tc % 2] = m_pre2

        if _STAGE >= 5:
            dbg_dump("sums2", sums2[:], (vec,))
            dbg_dump("ssq2", ssq2[:], (act,))

        if _STAGE < 6:
            for tc in range(TC):
                dma("sync", out[tc * 128:(tc + 1) * 128, 0:8], cp_sb2[:], st)

        @block.sync
        def _(sync):
            for f in b.q["sync"]:
                f(sync)
            sync.wait_ge(st.h, st.n)
            for _s in (sp2_sl[0], s_x1b):
                if _s.n > 0:
                    sync.wait_ge(_s.h, _s.n)

        @block.scalar
        def _(scalar):
            for f in b.q["scalar"]:
                f(scalar)

        @block.vector
        def _(vector):
            for f in b.q["vector"]:
                f(vector)

        @block.tensor
        def _(tensor):
            for f in b.q["tensor"]:
                f(tensor)

        @block.gpsimd
        def _(g):
            for f in b.q["gpsimd"]:
                f(g)

    return nc


_nc_cache = {}


def kernel(L0_attn_shot_embeddings, L0_attn_Wd, L0_attn_bd, L0_attn_Wi, L0_attn_bi,
           L0_attn_A_log, L0_attn_Wo, L0_attn_bo, L0_attn_Ws, L0_attn_bs,
           L0_attn_gamma, L0_attn_beta, L0_ffn_w_gate, L0_ffn_w_up, L0_ffn_w_down):
    x = np.ascontiguousarray(np.asarray(L0_attn_shot_embeddings, dtype=np.float32))
    Wd = np.ascontiguousarray(np.asarray(L0_attn_Wd, dtype=np.float32))
    bd = np.ascontiguousarray(np.asarray(L0_attn_bd, dtype=np.float32))
    Wi = np.ascontiguousarray(np.asarray(L0_attn_Wi, dtype=np.float32))
    bi = np.ascontiguousarray(np.asarray(L0_attn_bi, dtype=np.float32))
    A_log = np.ascontiguousarray(np.asarray(L0_attn_A_log, dtype=np.float32))
    Wo = np.ascontiguousarray(np.asarray(L0_attn_Wo, dtype=np.float32))
    bo = np.ascontiguousarray(np.asarray(L0_attn_bo, dtype=np.float32))
    Ws = np.ascontiguousarray(np.asarray(L0_attn_Ws, dtype=np.float32))
    bs = np.ascontiguousarray(np.asarray(L0_attn_bs, dtype=np.float32))
    wg = np.ascontiguousarray(np.asarray(L0_ffn_w_gate, dtype=np.float32))
    wu = np.ascontiguousarray(np.asarray(L0_ffn_w_up, dtype=np.float32))
    wdn = np.ascontiguousarray(np.asarray(L0_ffn_w_down, dtype=np.float32))

    key = ("main", _STAGE, _SERIAL, _SERIAL2, _SERIAL3, _GEXTRA, _ILW, _NOW, tuple(_DEBUG))
    if key not in _nc_cache:
        _nc_cache[key] = build_nc()
    nc = _nc_cache[key]

    cpool = np.tile(np.array([[0.0, 1.0, -1.0, 0.5, -0.5, EPS, 0.0, 0.0]], dtype=np.float32), (128, 1))
    ident = np.eye(128, dtype=ml_dtypes.bfloat16)
    ones1 = np.ones((1, 128), dtype=ml_dtypes.bfloat16)

    wg_p = np.concatenate([wg, np.zeros((NC_ * 512 - I, D), np.float32)], axis=0)
    wu_p = np.concatenate([wu, np.zeros((NC_ * 512 - I, D), np.float32)], axis=0)

    in_maps = []
    for c in range(NC_):
        bidx, k = c // 4, c % 4
        msel = np.zeros((128, 4, 8), np.float32)
        for j in range(4):
            msel[:, j, bidx * 4 + j] = 1.0
        ksel = np.zeros((128, 4), np.float32)
        ksel[:, k] = 1.0
        m = {
            "x": np.ascontiguousarray(x[bidx, k * T:(k + 1) * T, :]),
            "wd": Wd, "wi": Wi, "wo": Wo, "ws": Ws,
            "alog": np.ascontiguousarray(A_log.reshape(NL, NCH, 128).transpose(2, 1, 0)),
            "bd": np.ascontiguousarray(bd.reshape(NL, NCH, 128).transpose(2, 1, 0)),
            "bi": np.ascontiguousarray(bi.reshape(NL, NCH, 128).transpose(2, 1, 0)),
            "bo": np.ascontiguousarray(bo.reshape(NL, DCH, 128).transpose(2, 1, 0)),
            "bs": np.ascontiguousarray(bs.reshape(NL, DCH, 128).transpose(2, 1, 0)),
            "borow": np.ascontiguousarray(bo[1:2, :]),
            "bsrow": np.ascontiguousarray(bs[1:2, :]),
            "wg_sl": np.ascontiguousarray(wg_p[c * 512:(c + 1) * 512, :]),
            "wu_sl": np.ascontiguousarray(wu_p[c * 512:(c + 1) * 512, :]),
            "wdn_sl": np.ascontiguousarray(wdn[c * 128:(c + 1) * 128, :]),
            "cpool": cpool, "ident": ident, "ones1": ones1,
            "msel": msel, "ksel": ksel,
        }
        in_maps.append(m)

    res = run_bass_kernel_spmd(nc, in_maps, core_ids=list(range(NC_)))
    outf = np.empty((2, 8192, D), dtype=np.float32)
    for c in range(NC_):
        bidx, k = c // 4, c % 4
        outf[bidx, k * T:(k + 1) * T, :] = np.asarray(res.results[c]["out"])
    if _DEBUG:
        kernel._dbg = [{n: np.asarray(res.results[c]["dbg_" + n], dtype=np.float32)
                        for n in _DEBUG} for c in range(NC_)]
    return outf


# version 24
# speedup vs baseline: 1.0695x; 1.0695x over previous
import contextlib
import numpy as np
import ml_dtypes

import concourse.bass as bass
import concourse.mybir as mybir
from concourse.bass_utils import run_bass_kernel_spmd

F32 = mybir.dt.float32
BF16 = mybir.dt.bfloat16
U8 = mybir.dt.uint8
AF = mybir.ActivationFunctionType
Alu = mybir.AluOpType
AX = mybir.AxisListType.X

NC_ = 8
T = 2048
D = 1024
N = 256
I = 2816
TC = T // 128
DCH = D // 128
NCH = N // 128
ICH = I // 128
NL = 2
EPS = 1e-5
SCAN_SEG = 2048
QTHR = [0.25, 0.75, 1.25, 1.75, 2.5, 3.5, 5.0]
KB = 1024

_DEBUG = []
_STAGE = 99
_SERIAL = False
_SERIAL2 = ''
_SERIAL3 = ''
_GEXTRA = ''
_ILW = True
_NOW = False
_DEBUG_SHAPES = {
    "hs0": ([128, NCH * T], F32),
    "cp0": ([128, NCH * T], F32),
    "hsp0": ([128, NCH * T], F32),
    "summ0": ([128, NC_ * 4], F32),
    "pk0": ([128, 4], F32),
    "gin0": ([128, 4], F32),
    "gout0": ([NC_ * 128, 4], F32),
    "hs1": ([128, NCH * T], F32),
    "hsp1": ([128, NCH * T], F32),
    "h1fm": ([128, DCH * T], F32),
    "x1": ([T, D], F32),
    "xq": ([128, DCH * T], F32),
    "pre2": ([T, D], F32),
    "hrow15": ([128, I], F32),
    "hq15": ([128, I], F32),
    "hqfm15": ([128, ICH * 128], F32),
    "sums2": ([128, TC], F32),
    "ssq2": ([128, TC], F32),
}


class Sem:
    def __init__(self, h):
        self.h = h
        self.n = 0

    def mark(self):
        return (self, self.n)


class Build:
    def __init__(self):
        self.q = {e: [] for e in ("sync", "scalar", "vector", "tensor", "gpsimd")}

    def step(self, eng, fn, waits=()):
        frozen = []
        for s in waits:
            if isinstance(s, tuple):
                if s[1] > 0:
                    frozen.append((s[0].h, s[1]))
            elif s.n > 0:
                frozen.append((s.h, s.n))

        def run(engine, _frozen=frozen, _fn=fn):
            for h, v in _frozen:
                engine.wait_ge(h, v)
            _fn(engine)

        self.q[eng].append(run)


def build_nc():
    nc = bass.Bass("TRN2", target_bir_lowering=False, debug=False)

    x_in = nc.dram_tensor("x", [T, D], F32, kind="ExternalInput").ap()
    wd_in = nc.dram_tensor("wd", [NL, D, N], F32, kind="ExternalInput").ap()
    wi_in = nc.dram_tensor("wi", [NL, D, N], F32, kind="ExternalInput").ap()
    wo_in = nc.dram_tensor("wo", [NL, N, D], F32, kind="ExternalInput").ap()
    ws_in = nc.dram_tensor("ws", [NL, N, D], F32, kind="ExternalInput").ap()
    alog_in = nc.dram_tensor("alog", [128, NCH, NL], F32, kind="ExternalInput").ap()
    bd_in = nc.dram_tensor("bd", [128, NCH, NL], F32, kind="ExternalInput").ap()
    bi_in = nc.dram_tensor("bi", [128, NCH, NL], F32, kind="ExternalInput").ap()
    bo_in = nc.dram_tensor("bo", [128, DCH, NL], F32, kind="ExternalInput").ap()
    bs_in = nc.dram_tensor("bs", [128, DCH, NL], F32, kind="ExternalInput").ap()
    borow_in = nc.dram_tensor("borow", [1, D], F32, kind="ExternalInput").ap()
    bsrow_in = nc.dram_tensor("bsrow", [1, D], F32, kind="ExternalInput").ap()
    wg_sl = nc.dram_tensor("wg_sl", [512, D], F32, kind="ExternalInput").ap()
    wu_sl = nc.dram_tensor("wu_sl", [512, D], F32, kind="ExternalInput").ap()
    wdn_sl = nc.dram_tensor("wdn_sl", [128, I], F32, kind="ExternalInput").ap()
    cpool = nc.dram_tensor("cpool", [128, 8], F32, kind="ExternalInput").ap()
    ident_in = nc.dram_tensor("ident", [128, 128], BF16, kind="ExternalInput").ap()
    ones1_in = nc.dram_tensor("ones1", [1, 128], BF16, kind="ExternalInput").ap()
    msel_in = nc.dram_tensor("msel", [128, 4, 8], F32, kind="ExternalInput").ap()
    ksel_in = nc.dram_tensor("ksel", [128, 4], F32, kind="ExternalInput").ap()

    out = nc.dram_tensor("out", [T, D], F32, kind="ExternalOutput").ap()
    dbg = {}
    for name in _DEBUG:
        shp, dt_ = _DEBUG_SHAPES[name]
        dbg[name] = nc.dram_tensor("dbg_" + name, shp, dt_, kind="ExternalOutput").ap()

    gin_wg = nc.dram_tensor("gin_wg", [128, DCH * 512], BF16).ap()
    gout_wg = nc.dram_tensor("gout_wg", [NC_ * 128, DCH * 512], BF16).ap()
    gin_wu = nc.dram_tensor("gin_wu", [128, DCH * 512], BF16).ap()
    gout_wu = nc.dram_tensor("gout_wu", [NC_ * 128, DCH * 512], BF16).ap()
    gin_wd = nc.dram_tensor("gin_wd", [I, 128], BF16).ap()
    gout_wd = nc.dram_tensor("gout_wd", [NC_ * I, 128], BF16).ap()
    wdq2 = nc.dram_tensor("wdq2", [I, D], BF16).ap()
    gin_s = [nc.dram_tensor(f"gin_s{l}", [128, 4], F32).ap() for l in range(NL)]
    gout_s = [nc.dram_tensor(f"gout_s{l}", [NC_ * 128, 4], F32).ap() for l in range(NL)]
    pre2_d = nc.dram_tensor("pre2_d", [T, D], BF16).ap()
    x1_d = nc.dram_tensor("x1_d", [T, D], BF16).ap()

    b = Build()

    with contextlib.ExitStack() as stack:
        SB = lambda *a, **k: stack.enter_context(nc.sbuf_tensor(*a, **k))
        PSUM = lambda *a: stack.enter_context(nc.psum_tensor(*a))

        arena = SB("arena", [128, 128 * KB], U8)

        def AV(off_kb, nbytes, shape, dt):
            v = arena[:, int(off_kb * KB):int(off_kb * KB) + int(nbytes)].bitcast(dt)
            n = int(np.prod(shape))
            assert v.shape[1] >= n, (shape, v.shape)
            v = v[:, 0:n]
            if len(shape) > 1:
                sh = " ".join(f"a{i}" for i in range(len(shape)))
                kw = {f"a{i}": s for i, s in enumerate(shape)}
                v = v.rearrange(f"p ({sh}) -> p {sh}", **kw)
            return v

        x_bf = AV(0, 32 * KB, [TC, D], BF16)
        a_tmp = AV(0, 8 * KB, [T], F32)
        a_dl = AV(8, 8 * KB, [T], F32)
        a_dc = AV(16, 8 * KB, [T], F32)
        a_dr = AV(24, 8 * KB, [T], F32)
        wgst = AV(0, 16 * KB, [2, DCH, 512], BF16)
        wust = AV(16, 16 * KB, [2, DCH, 512], BF16)
        x_fm = AV(32, 32 * KB, [DCH, T], BF16)
        n_au = AV(32, 4 * KB, [D], F32)
        n_k1 = AV(36, 2 * KB, [D], BF16)
        n_k2 = AV(38, 2 * KB, [D], BF16)
        n_t3 = AV(40, 2 * KB, [D], BF16)
        n_sg = AV(42, 2 * KB, [D], BF16)
        n_dst = AV(44, 2 * KB, [D], BF16)
        n_au2 = AV(46, 4 * KB, [D], F32)
        n_sg2 = AV(50, 2 * KB, [D], BF16)
        n_dst2 = AV(52, 2 * KB, [D], BF16)
        x1b = AV(54, 2 * KB, [D], BF16)
        w_k1 = AV(32, 5632, [I], BF16)
        w_k2 = AV(37.5, 5632, [I], BF16)
        w_t3 = AV(43, 5632, [I], BF16)
        w_sg = AV(48.5, 5632, [I], BF16)
        w_dst = AV(54, 5632, [I], BF16)
        w_dst2 = AV(88, 5632, [I], BF16)
        h1_fm = AV(64, 32 * KB, [DCH, T], BF16)
        xres = AV(64, 8 * KB, [2, D], F32)
        pre1 = AV(72, 4 * KB, [D], F32)
        x1f = AV(76, 4 * KB, [D], F32)
        w_au = AV(64, 11264, [I], F32)
        sqscr = AV(80, 2 * KB, [D], BF16)
        hq_fm = AV(82, 5632, [ICH, 128], BF16)
        pre2s = AV(94, 2 * KB, [D], BF16)
        x1rA = AV(76, 4 * KB, [2, D], BF16)
        x1rB = AV(59.5, 2 * KB, [D], BF16)
        x1r_s = lambda i: x1rA[:, i, :] if i < 2 else x1rB[:]
        pre2r = AV(72, 4 * KB, [2, D], BF16)
        outb = AV(64, 8 * KB, [2, D], F32)
        hs_sb = AV(96, 16 * KB, [NCH, T], F32)
        cp_sb = AV(112, 16 * KB, [NCH, T], F32)
        xq_fm = AV(96, 32 * KB, [DCH, T], BF16)

        wdwi = SB("wdwi", [128, 2, DCH, N], BF16)
        wo_sb = SB("wo_sb", [128, NCH, NL, D], BF16)
        ws_sb = SB("ws_sb", [128, NCH, NL, D], BF16)
        wdst = SB("wdst", [128, 3, D], BF16)
        d_au = SB("d_au", [128, D], F32)
        d_k1 = SB("d_k1", [128, D], BF16)
        d_k2 = SB("d_k2", [128, D], BF16)
        d_t3 = SB("d_t3", [128, D], BF16)
        d_sg = SB("d_sg", [128, D], BF16)
        wTst = SB("wTst_sb", [128, 8, 128], BF16)
        hrow = SB("hrow", [128, I], F32)
        hsp = SB("hsp", [128, NCH, T], BF16)
        zeros_b = SB("zeros_b", [128, T], BF16)
        q_bm = SB("q_bm", [128, 176], F32)
        q_rr = SB("q_rr", [128, 176], F32)
        q_sh = SB("q_sh", [128, 176], F32)
        cp_sb2 = SB("cpool_sb", [128, 8], F32)
        identb = SB("identb", [128, 128], BF16)
        ones1 = SB("ones1_sb", [1, 128], BF16)
        msel = SB("msel_sb", [128, 4, 8], F32)
        ksel = SB("ksel_sb", [128, 4], F32)
        alog_f = SB("alog_f", [128, NCH, NL], F32)
        sa_t = SB("sa_t", [128, NCH, NL], F32)
        nsa = SB("nsa", [128, NCH, NL], F32)
        bd_f = SB("bd_f", [128, NCH, NL], F32)
        bi_f = SB("bi_f", [128, NCH, NL], F32)
        bo_f = SB("bo_f", [128, DCH, NL], F32)
        bs_f = SB("bs_f", [128, DCH, NL], F32)
        summ = SB("summ", [128, NC_, 4], F32)
        pk_sb = SB("pk_sb", [128, 4], F32)
        jsel = SB("jsel", [128, NC_], F32)
        sj = SB("sj", [128, NCH, 5], F32)
        s4sel = SB("s4sel", [128, 4], F32)
        hin = SB("hin", [128, NCH], F32)
        st4b = SB("st4b", [128, NCH], BF16)
        bvec = SB("bvec", [128, DCH], F32)
        bvec0 = SB("bvec0", [128, DCH], F32)
        brow = SB("brow", [1, D], BF16)
        r1row = SB("r1row", [1, D], BF16)
        rowf = SB("rowf", [1, D], BF16)
        sums2 = SB("sums2", [128, TC], F32)
        ssq2 = SB("ssq2", [128, TC], F32)
        st1 = SB("st1", [128, 8], F32)
        scan_carry = SB("scan_carry", [128, 2], F32)

        CONSTS = [0.0, 1.0, -1.0, 0.5, -0.5, EPS]
        for i, v in enumerate(CONSTS):
            nc.const_aps.register(F32, v, cp_sb2[:, i:i + 1])

        psA = PSUM("psA", [128, 2048], F32)
        psB = PSUM("psB", [128, 1536], F32)
        psT = PSUM("psT", [128, 1024], BF16)

        sem_names = ("ld", "gp", "cc", "pe", "act", "vec", "st", "ws")
        S_ = {s: Sem(stack.enter_context(nc.semaphore(s))) for s in sem_names}
        ld, gp, cc, pe, act, vec, st, ws = (S_[k] for k in sem_names)
        sg_sl = [Sem(stack.enter_context(nc.semaphore(f"sgw{i}"))) for i in range(2)]
        su_sl = [Sem(stack.enter_context(nc.semaphore(f"suw{i}"))) for i in range(2)]
        sd_sl = [Sem(stack.enter_context(nc.semaphore(f"sdw{i}"))) for i in range(3)]
        sxr_sl = [Sem(stack.enter_context(nc.semaphore(f"sxr{i}"))) for i in range(2)]
        sx1_sl = [Sem(stack.enter_context(nc.semaphore(f"sx1{i}"))) for i in range(3)]
        sp2_sl = [Sem(stack.enter_context(nc.semaphore(f"sp2{i}"))) for i in range(2)]
        s_summ = Sem(stack.enter_context(nc.semaphore("s_summ")))
        s_wdwi = Sem(stack.enter_context(nc.semaphore("s_wdwi")))
        s_wh = Sem(stack.enter_context(nc.semaphore("s_wh")))
        s_x1b = Sem(stack.enter_context(nc.semaphore("s_x1b")))

        block = stack.enter_context(nc.Block())

        def dma(eng_name, dst, src, sem, waits=()):
            if _SERIAL or "dma" in _SERIAL2:
                waits = tuple(waits) + (pe, act, vec, ws, ld, gp, st)
            def f(e, _d=dst, _s=src, _sem=sem):
                e.dma_start(out=_d, in_=_s).then_inc(_sem.h, 16)
            b.step(eng_name, f, waits)
            sem.n += 16

        def op(eng_name, fn, sem=None, waits=(), tag=""):
            _sem = sem
            if _SERIAL or eng_name in _SERIAL2 or (tag and tag in _SERIAL3):
                waits = tuple(waits) + (pe, act, vec, ws, ld, gp, st)
            def f(e, _fn=fn):
                r = _fn(e)
                if _sem is not None:
                    r.then_inc(_sem.h, 1)
            b.step(eng_name, f, waits)
            if _sem is not None:
                _sem.n += 1

        def dbg_dump(name, src, waits):
            if name in dbg:
                dma("gpsimd", dbg[name], src, gp, waits)

        dma("sync", cp_sb2[:], cpool, ld)
        dma("sync", identb[:], ident_in, ld)
        dma("sync", ones1[:], ones1_in, ld)
        dma("sync", msel[:], msel_in, ld)
        dma("sync", ksel[:], ksel_in, ld)
        dma("sync", alog_f[:], alog_in, ld)
        dma("sync", bd_f[:], bd_in, ld)
        dma("sync", bi_f[:], bi_in, ld)
        dma("sync", bo_f[:], bo_in, ld)
        dma("sync", bs_f[:], bs_in, ld)
        dma("gpsimd", brow[:], borow_in, gp)
        dma("gpsimd", rowf[:], bsrow_in, gp)

        dma("gpsimd", wdwi[:, 0, :, :], wd_in[0].rearrange("(a p) n -> p a n", p=128), gp)
        dma("gpsimd", wdwi[:, 1, :, :], wi_in[0].rearrange("(a p) n -> p a n", p=128), gp)
        for l_ in range(NL):
            dma("gpsimd", wo_sb[:, :, l_, :], wo_in[l_].rearrange("(c p) d -> p c d", p=128), gp)
            dma("gpsimd", ws_sb[:, :, l_, :], ws_in[l_].rearrange("(c p) d -> p c d", p=128), gp)
        dma("gpsimd", x_bf[:], x_in.rearrange("(c p) d -> p c d", p=128), gp)

        op("vector", lambda e: e.tensor_tensor(out=brow[:], in0=brow[:], in1=rowf[:], op=Alu.add), vec, (gp,))
        op("scalar", lambda e: e.activation(out=sa_t[:], in_=alog_f[:], func=AF.Exp), act, (ld,))
        op("scalar", lambda e: e.activation(out=sa_t[:], in_=sa_t[:], func=AF.Ln, bias=1.0), act, (act,))
        op("vector", lambda e: e.tensor_single_scalar(out=nsa[:], in_=sa_t[:], scalar=-1.0, op=Alu.mult), vec, (act,))
        op("vector", lambda e: e.memset(zeros_b[:], 0.0), vec)

        for tc in range(TC):
            def mk(tc=tc):
                def f(e):
                    last = None
                    for dch in range(DCH):
                        last = e.transpose(psT[:, dch * 128:(dch + 1) * 128],
                                           x_bf[:, tc, dch * 128:(dch + 1) * 128], identb[:])
                    return last
                return f
            op("tensor", mk(), pe, (gp, ld) if tc == 0 else (act,))
            op("scalar", (lambda tc=tc: (lambda e: e.activation(
                out=x_fm[:, :, tc * 128:(tc + 1) * 128],
                in_=psT[:].rearrange("p (a t) -> p a t", t=128), func=AF.Identity)))(), act, (pe,))

        m_xfm_ready = act.mark()

        def emit_quant(src, dst, C, bufs, waits=(), sink=None):
            au, k1, k2, t3, sg = bufs
            if sink is None:
                OP = op
            else:
                def OP(eng, fn, sem=None, w=(), tag=""):
                    sink.append(lambda: op(eng, fn, sem, w, tag))
            nb = C // 16
            s3 = src.rearrange("p (n b) -> p n b", b=16)
            OP("vector", lambda e: e.tensor_reduce(out=q_bm[:, 0:nb].unsqueeze(2), in_=s3,
                                                   axis=AX, op=Alu.max, apply_absolute_value=True),
               vec, tuple(waits) + (vec,))
            OP("vector", lambda e: e.reciprocal(out=q_rr[:, 0:nb], in_=q_bm[:, 0:nb]), vec, (vec,))
            OP("vector", lambda e: e.tensor_scalar(out=q_rr[:, 0:nb], in0=q_rr[:, 0:nb], scalar1=6.0,
                                                   scalar2=1e12, op0=Alu.mult, op1=Alu.min), vec, (vec,))
            OP("vector", lambda e: e.tensor_scalar(out=q_sh[:, 0:nb], in0=q_bm[:, 0:nb], scalar1=1.0 / 6.0,
                                                   scalar2=1e-12, op0=Alu.mult, op1=Alu.max), vec, (vec,))
            rrb = q_rr[:, 0:nb].unsqueeze(2).broadcast_to([128, nb, 16])
            shb = q_sh[:, 0:nb].unsqueeze(2).broadcast_to([128, nb, 16])
            OP("vector", (lambda au=au: (lambda e: e.tensor_tensor(
                out=au[:, 0:C].rearrange("p (n b) -> p n b", b=16),
                in0=s3, in1=rrb, op=Alu.mult)))(), vec, (vec, act))
            m_srcfree_v = vec.mark() if sink is None else None
            OP("scalar", (lambda au=au: (lambda e: e.activation(out=au[:, 0:C], in_=au[:, 0:C], func=AF.Abs)))(), act, (vec,))
            OP("scalar", (lambda sg=sg: (lambda e: e.activation(out=sg[:, 0:C], in_=src, func=AF.Sign)))(),
               act, tuple(waits) + (vec,))
            m_srcfree_a = act.mark() if sink is None else None
            OP("vector", (lambda au=au, k1=k1: (lambda e: e.tensor_single_scalar(
                out=k1[:, 0:C], in_=au[:, 0:C], scalar=QTHR[0], op=Alu.is_gt)))(), vec, (act, pe))
            ks = [k1, k2]
            for i, t in enumerate(QTHR[1:]):
                sk, dk = ks[i % 2], ks[(i + 1) % 2]
                OP("vector", (lambda t=t, sk=sk, dk=dk, au=au: (lambda e: e.scalar_tensor_tensor(
                    out=dk[:, 0:C], in0=au[:, 0:C], scalar=float(t), in1=sk[:, 0:C],
                    op0=Alu.is_gt, op1=Alu.add)))(), vec, (vec,))
            kf = ks[len(QTHR[1:]) % 2]
            OP("vector", (lambda kf=kf, k2=k2: (lambda e: e.tensor_single_scalar(
                out=k2[:, 0:C], in_=kf[:, 0:C], scalar=0.5, op=Alu.mult)))(), vec, (vec,))
            OP("vector", (lambda kf=kf, t3=t3: (lambda e: e.tensor_single_scalar(
                out=t3[:, 0:C], in_=kf[:, 0:C], scalar=-2.0, op=Alu.add)))(), vec, (vec,))
            OP("vector", (lambda k2=k2, t3=t3: (lambda e: e.tensor_tensor(
                out=k2[:, 0:C], in0=k2[:, 0:C], in1=t3[:, 0:C], op=Alu.max)))(), vec, (vec,))
            OP("vector", (lambda kf=kf, t3=t3: (lambda e: e.tensor_scalar(
                out=t3[:, 0:C], in0=kf[:, 0:C], scalar1=2.0, scalar2=-8.0,
                op0=Alu.mult, op1=Alu.add)))(), vec, (vec,))
            OP("vector", (lambda k2=k2, t3=t3: (lambda e: e.tensor_tensor(
                out=k2[:, 0:C], in0=k2[:, 0:C], in1=t3[:, 0:C], op=Alu.max)))(), vec, (vec,))
            OP("vector", (lambda sg=sg, t3=t3: (lambda e: e.tensor_tensor(
                out=t3[:, 0:C].rearrange("p (n b) -> p n b", b=16),
                in0=sg[:, 0:C].rearrange("p (n b) -> p n b", b=16),
                in1=shb, op=Alu.mult)))(), vec, (vec, act))
            OP("vector", (lambda k2=k2, t3=t3: (lambda e: e.tensor_tensor(
                out=dst, in0=k2[:, 0:C], in1=t3[:, 0:C], op=Alu.mult)))(), vec, (vec, pe))
            return (m_srcfree_v, m_srcfree_a)

        def emit_quant_A(src, C, au, sg, bmo, waits=()):
            nb = C // 16
            s3 = src.rearrange("p (n b) -> p n b", b=16)
            op("vector", (lambda s3=s3, nb=nb, bmo=bmo: (lambda e: e.tensor_reduce(
                out=q_bm[:, bmo:bmo + nb].unsqueeze(2), in_=s3,
                axis=AX, op=Alu.max, apply_absolute_value=True)))(), vec, tuple(waits) + (vec,))
            op("vector", (lambda nb=nb, bmo=bmo: (lambda e: e.reciprocal(
                out=q_rr[:, bmo:bmo + nb], in_=q_bm[:, bmo:bmo + nb])))(), vec, (vec,))
            op("vector", (lambda nb=nb, bmo=bmo: (lambda e: e.tensor_scalar(
                out=q_rr[:, bmo:bmo + nb], in0=q_rr[:, bmo:bmo + nb], scalar1=6.0,
                scalar2=1e12, op0=Alu.mult, op1=Alu.min)))(), vec, (vec,))
            op("vector", (lambda nb=nb, bmo=bmo: (lambda e: e.tensor_scalar(
                out=q_sh[:, bmo:bmo + nb], in0=q_bm[:, bmo:bmo + nb], scalar1=1.0 / 6.0,
                scalar2=1e-12, op0=Alu.mult, op1=Alu.max)))(), vec, (vec,))
            rrb = q_rr[:, bmo:bmo + nb].unsqueeze(2).broadcast_to([128, nb, 16])
            op("vector", (lambda au=au, s3=s3, rrb=rrb, C=C: (lambda e: e.tensor_tensor(
                out=au[:, 0:C].rearrange("p (n b) -> p n b", b=16),
                in0=s3, in1=rrb, op=Alu.mult)))(), vec, (vec, act))
            op("scalar", (lambda au=au, C=C: (lambda e: e.activation(
                out=au[:, 0:C], in_=au[:, 0:C], func=AF.Abs)))(), act, (vec,))
            op("scalar", (lambda sg=sg, src=src, C=C: (lambda e: e.activation(
                out=sg[:, 0:C], in_=src, func=AF.Sign)))(), act, tuple(waits) + (vec,))
            return act.mark()

        def emit_quant_B(dst, C, au, sg, bmo, m_absg):
            nb = C // 16
            shb = q_sh[:, bmo:bmo + nb].unsqueeze(2).broadcast_to([128, nb, 16])
            k1, k2, t3 = n_k1, n_k2, n_t3
            op("vector", (lambda au=au, k1=k1, C=C: (lambda e: e.tensor_single_scalar(
                out=k1[:, 0:C], in_=au[:, 0:C], scalar=QTHR[0], op=Alu.is_gt)))(), vec, (m_absg,))
            ks = [k1, k2]
            for i, t in enumerate(QTHR[1:]):
                sk, dk = ks[i % 2], ks[(i + 1) % 2]
                op("vector", (lambda t=t, sk=sk, dk=dk, au=au, C=C: (lambda e: e.scalar_tensor_tensor(
                    out=dk[:, 0:C], in0=au[:, 0:C], scalar=float(t), in1=sk[:, 0:C],
                    op0=Alu.is_gt, op1=Alu.add)))(), vec, (vec,))
            kf = ks[len(QTHR[1:]) % 2]
            op("vector", (lambda kf=kf, k2=k2, C=C: (lambda e: e.tensor_single_scalar(
                out=k2[:, 0:C], in_=kf[:, 0:C], scalar=0.5, op=Alu.mult)))(), vec, (vec,))
            op("vector", (lambda kf=kf, t3=t3, C=C: (lambda e: e.tensor_single_scalar(
                out=t3[:, 0:C], in_=kf[:, 0:C], scalar=-2.0, op=Alu.add)))(), vec, (vec,))
            op("vector", (lambda k2=k2, t3=t3, C=C: (lambda e: e.tensor_tensor(
                out=k2[:, 0:C], in0=k2[:, 0:C], in1=t3[:, 0:C], op=Alu.max)))(), vec, (vec,))
            op("vector", (lambda kf=kf, t3=t3, C=C: (lambda e: e.tensor_scalar(
                out=t3[:, 0:C], in0=kf[:, 0:C], scalar1=2.0, scalar2=-8.0,
                op0=Alu.mult, op1=Alu.add)))(), vec, (vec,))
            op("vector", (lambda k2=k2, t3=t3, C=C: (lambda e: e.tensor_tensor(
                out=k2[:, 0:C], in0=k2[:, 0:C], in1=t3[:, 0:C], op=Alu.max)))(), vec, (vec,))
            op("vector", (lambda sg=sg, t3=t3, shb=shb, C=C: (lambda e: e.tensor_tensor(
                out=t3[:, 0:C].rearrange("p (n b) -> p n b", b=16),
                in0=sg[:, 0:C].rearrange("p (n b) -> p n b", b=16),
                in1=shb, op=Alu.mult)))(), vec, (vec, act))
            op("vector", (lambda k2=k2, t3=t3, dst=dst, C=C: (lambda e: e.tensor_tensor(
                out=dst, in0=k2[:, 0:C], in1=t3[:, 0:C], op=Alu.mult)))(), vec, (vec, pe))
            return vec.mark()

        NBUF = (n_au[:], n_k1[:], n_k2[:], n_t3[:], n_sg[:])
        WBUF = (w_au[:], w_k1[:], w_k2[:], w_t3[:], w_sg[:])
        DBUF = (d_au[:], d_k1[:], d_k2[:], d_t3[:], d_sg[:])

        wq_jobs = []
        for s in range(4):
            wq_jobs.append(("wg", s, D, wg_sl[s * 128:(s + 1) * 128, :]))
        for s in range(4):
            wq_jobs.append(("wu", s, D, wu_sl[s * 128:(s + 1) * 128, :]))
        wq_jobs.append(("wd", 0, 1024, wdn_sl[:, 0:1024]))
        wq_jobs.append(("wd", 1, 1024, wdn_sl[:, 1024:2048]))
        wq_jobs.append(("wd", 2, I - 2048, wdn_sl[:, 2048:I]))
        if _STAGE < 2 or _NOW:
            wq_jobs = []

        def wjob_thunks(job):
            kind, s, C, src_ap = job
            th = []
            th.append(lambda: dma("sync", hrow[:, 0:C], src_ap, s_wh, (vec,)))
            emit_quant(hrow[:, 0:C], d_k1[:, 0:C], C, DBUF, waits=(s_wh,), sink=th)
            ntile = C // 128
            def mkT(ntile=ntile):
                def f(e):
                    last = None
                    for j in range(ntile):
                        last = e.transpose(psT[:, j * 128:(j + 1) * 128],
                                           d_k1[:, j * 128:(j + 1) * 128], identb[:])
                    return last
                return f
            th.append(lambda: op("tensor", mkT(), pe, (vec, act), tag="wT"))
            th.append(lambda: op("scalar", (lambda ntile=ntile: (lambda e: e.activation(
                out=wTst[:, 0:ntile, :],
                in_=psT[:, 0:ntile * 128].rearrange("p (a t) -> p a t", t=128), func=AF.Identity)))(), act, (pe, ws)))
            if kind == "wg":
                dst_ap = gin_wg.rearrange("p (a s t) -> p a s t", a=DCH, s=4)[:, :, s, :]
            elif kind == "wu":
                dst_ap = gin_wu.rearrange("p (a s t) -> p a s t", a=DCH, s=4)[:, :, s, :]
            else:
                dst_ap = gin_wd[s * 1024:s * 1024 + C, :].rearrange("(a p) j -> p a j", p=128)
            th.append(lambda: dma("sync", dst_ap, wTst[:, 0:ntile, :], ws, (act,)))
            return th

        _wq_thunks = []
        for _j in wq_jobs:
            _wq_thunks.extend(wjob_thunks(_j))
        if not _ILW:
            for _t in _wq_thunks:
                _t()
            _wq_thunks = []

        def emit_wjob_maybe(n=1):
            for _ in range(2 * n):
                if _wq_thunks:
                    _wq_thunks.pop(0)()

        def emit_wgather():
            for gi, go in (((gin_wg, gout_wg), (gin_wu, gout_wu), (gin_wd, gout_wd)) if _STAGE >= 2 else ()):
                def mkcc(gi=gi, go=go):
                    def f(e):
                        return e.collective_compute("AllGather", Alu.bypass,
                                                    replica_groups=[list(range(NC_))],
                                                    ins=[gi], outs=[go])
                    return f
                op("gpsimd", mkcc(), cc, (ws,))
            for cpr in (range(NC_) if _STAGE >= 2 else ()):
                dma("gpsimd", wdq2[:, cpr * 128:(cpr + 1) * 128],
                    gout_wd[cpr * I:(cpr + 1) * I, :], gp, (cc,))

        m_rhs = m_xfm_ready
        m_psA = None
        m_aDl = None
        m_aDc = None
        m_psB = None
        m_wdwi = gp.mark()
        for l in range(NL if _STAGE >= 3 else 0):
            rhs = x_fm if l == 0 else h1_fm
            if l == 1:
                dma("gpsimd", wdwi[:, 0, :, :], wd_in[1].rearrange("(a p) n -> p a n", p=128), s_wdwi, (pe,))
                dma("gpsimd", wdwi[:, 1, :, :], wi_in[1].rearrange("(a p) n -> p a n", p=128), s_wdwi, (pe,))
                m_wdwi = s_wdwi.mark()

            for nch in range(NCH):
                def mk_d(l=l, nch=nch, rhs=rhs):
                    def f(e):
                        last = None
                        for tt in range(4):
                            for kk in range(DCH):
                                last = e.matmul(psA[:, tt * 512:(tt + 1) * 512],
                                                wdwi[:, 0, kk, nch * 128:(nch + 1) * 128],
                                                rhs[:, kk, tt * 512:(tt + 1) * 512],
                                                start=(kk == 0), stop=(kk == DCH - 1))
                        return last
                    return f
                op("tensor", mk_d(), pe,
                   (m_rhs, m_wdwi) + ((m_psA,) if m_psA is not None else ()), tag="dmm")
                m_dmm = pe.mark()
                emit_wjob_maybe(2)
                op("scalar", (lambda l=l, nch=nch: (lambda e: e.activation(
                    out=a_tmp[:], in_=psA[:], func=AF.Exp,
                    bias=bd_f[:, nch, l].unsqueeze(1))))(), act, (m_dmm,))
                m_psA = act.mark()
                emit_wjob_maybe(2)
                op("scalar", lambda e: e.activation(out=a_dl[:], in_=a_tmp[:], func=AF.Ln, bias=1.0), act,
                   ((m_aDl,) if m_aDl is not None else ()))
                m_ln = act.mark()
                emit_wjob_maybe(2)
                op("scalar", (lambda l=l, nch=nch: (lambda e: e.activation(
                    out=a_dc[:], in_=a_dl[:], func=AF.Exp,
                    scale=nsa[:, nch, l].unsqueeze(1))))(), act,
                   ((m_aDc,) if m_aDc is not None else ()))
                m_decay = act.mark()
                emit_wjob_maybe(2)
                for half in range(2):
                    def mk_u(l=l, nch=nch, half=half, rhs=rhs):
                        def f(e):
                            last = None
                            for tt in range(2):
                                t0 = half * 2 + tt
                                for kk in range(DCH):
                                    last = e.matmul(psB[:, tt * 512:(tt + 1) * 512],
                                                    wdwi[:, 1, kk, nch * 128:(nch + 1) * 128],
                                                    rhs[:, kk, t0 * 512:(t0 + 1) * 512],
                                                    start=(kk == 0), stop=(kk == DCH - 1))
                            return last
                        return f
                    op("tensor", mk_u(), pe,
                       (m_rhs, m_wdwi) + ((m_psB,) if m_psB is not None else ()), tag="umm")
                    emit_wjob_maybe(2)
                    m_umm = pe.mark()
                    op("vector", (lambda l=l, nch=nch, half=half: (lambda e: e.scalar_tensor_tensor(
                        out=a_dr[:, half * 1024:(half + 1) * 1024],
                        in0=psB[:, 0:1024], scalar=bi_f[:, nch, l].unsqueeze(1),
                        in1=a_dl[:, half * 1024:(half + 1) * 1024],
                        op0=Alu.add, op1=Alu.mult)))(), vec, (m_umm, m_ln))
                    m_psB = vec.mark()
                    emit_wjob_maybe(2)
                m_aDl = m_psB
                nseg = T // SCAN_SEG
                for seg in range(nseg):
                    s0, s1 = seg * SCAN_SEG, (seg + 1) * SCAN_SEG
                    ini_h = 0.0 if seg == 0 else scan_carry[:, 0:1]
                    ini_c = 1.0 if seg == 0 else scan_carry[:, 1:2]
                    op("vector", (lambda s0=s0, s1=s1, ini=ini_h, nch=nch: (lambda e: e.tensor_tensor_scan(
                        out=hs_sb[:, nch, s0:s1], data0=a_dc[:, s0:s1], data1=a_dr[:, s0:s1],
                        initial=ini, op0=Alu.mult, op1=Alu.add)))(), vec, (m_decay,))
                    op("vector", (lambda s0=s0, s1=s1, ini=ini_c, nch=nch: (lambda e: e.tensor_tensor_scan(
                        out=cp_sb[:, nch, s0:s1], data0=a_dc[:, s0:s1], data1=zeros_b[:, s0:s1],
                        initial=ini, op0=Alu.mult, op1=Alu.add)))(), vec, ())
                    emit_wjob_maybe(2)
                    if seg < nseg - 1:
                        op("vector", (lambda s1=s1, nch=nch: (lambda e: e.tensor_copy(
                            out=scan_carry[:, 0:1], in_=hs_sb[:, nch, s1 - 1:s1])))(), vec, (vec,))
                        op("vector", (lambda s1=s1, nch=nch: (lambda e: e.tensor_copy(
                            out=scan_carry[:, 1:2], in_=cp_sb[:, nch, s1 - 1:s1])))(), vec, (vec,))
                m_aDc = vec.mark()
                op("vector", (lambda nch=nch: (lambda e: e.tensor_copy(
                    out=pk_sb[:, nch:nch + 1], in_=cp_sb[:, nch, T - 1:T])))(), vec, (vec,))
                op("vector", (lambda nch=nch: (lambda e: e.tensor_copy(
                    out=pk_sb[:, 2 + nch:3 + nch], in_=hs_sb[:, nch, T - 1:T])))(), vec, (vec,))
                emit_wjob_maybe(2)

            if l == 1:
                dbg_dump("hs1", hs_sb[:].rearrange("p a t -> p (a t)"), (vec,))
            else:
                dbg_dump("hs0", hs_sb[:].rearrange("p a t -> p (a t)"), (vec,))
                dbg_dump("cp0", cp_sb[:].rearrange("p a t -> p (a t)"), (vec,))

            dma("sync", gin_s[l], pk_sb[:], ws, (vec,))
            def mkccs(l=l):
                def f(e):
                    return e.collective_compute("AllGather", Alu.bypass,
                                                replica_groups=[list(range(NC_))],
                                                ins=[gin_s[l]], outs=[gout_s[l]])
                return f
            op("gpsimd", mkccs(), cc, (ws,))
            dma("sync", summ[:], gout_s[l].rearrange("(c p) f -> p c f", p=128), s_summ, (cc, vec))
            m_summ = s_summ.mark()

            for nch in range(NCH):
                op("vector", (lambda nch=nch: (lambda e: e.memset(sj[:, nch, 0:1], 0.0)))(), vec, (m_summ,))
                for j in range(4):
                    op("vector", (lambda j=j, nch=nch: (lambda e: e.tensor_tensor(
                        out=jsel[:], in0=summ[:, :, nch], in1=msel[:, j, :], op=Alu.mult)))(), vec, (vec,))
                    op("vector", lambda e: e.tensor_reduce(out=st1[:, 0:1], in_=jsel[:], axis=AX, op=Alu.add), vec, (vec,))
                    op("vector", (lambda j=j, nch=nch: (lambda e: e.tensor_tensor(
                        out=jsel[:], in0=summ[:, :, 2 + nch], in1=msel[:, j, :], op=Alu.mult)))(), vec, (vec,))
                    op("vector", lambda e: e.tensor_reduce(out=st1[:, 1:2], in_=jsel[:], axis=AX, op=Alu.add), vec, (vec,))
                    op("vector", (lambda j=j, nch=nch: (lambda e: e.tensor_tensor(
                        out=st1[:, 2:3], in0=st1[:, 0:1], in1=sj[:, nch, j:j + 1], op=Alu.mult)))(), vec, (vec,))
                    op("vector", (lambda j=j, nch=nch: (lambda e: e.tensor_tensor(
                        out=sj[:, nch, j + 1:j + 2], in0=st1[:, 2:3], in1=st1[:, 1:2], op=Alu.add)))(), vec, (vec,))
                    emit_wjob_maybe(2)
                op("vector", (lambda nch=nch: (lambda e: e.tensor_tensor(
                    out=s4sel[:], in0=sj[:, nch, 0:4], in1=ksel[:], op=Alu.mult)))(), vec, (vec,))
                op("vector", (lambda nch=nch: (lambda e: e.tensor_reduce(
                    out=hin[:, nch:nch + 1], in_=s4sel[:], axis=AX, op=Alu.add)))(), vec, (vec,))
                op("vector", (lambda nch=nch: (lambda e: e.tensor_copy(
                    out=st4b[:, nch:nch + 1], in_=sj[:, nch, 4:5])))(), vec, (vec,))
                op("vector", (lambda nch=nch: (lambda e: e.scalar_tensor_tensor(
                    out=hsp[:, nch, :], in0=cp_sb[:, nch, :], scalar=hin[:, nch:nch + 1],
                    in1=hs_sb[:, nch, :], op0=Alu.mult, op1=Alu.add)))(), vec, (vec,))
            m_hsp = vec.mark()
            emit_wjob_maybe(4)

            if l == 1:
                dbg_dump("hsp1", hsp[:].rearrange("p a t -> p (a t)"), (vec,))
            else:
                dbg_dump("hsp0", hsp[:].rearrange("p a t -> p (a t)"), (vec,))
                dbg_dump("summ0", summ[:].rearrange("p c f -> p (c f)"), (vec,))
                dbg_dump("pk0", pk_sb[:], (vec,))
                if "gin0" in dbg:
                    dma("gpsimd", dbg["gin0"], gin_s[0], gp, (cc,))
                if "gout0" in dbg:
                    dma("gpsimd", dbg["gout0"], gout_s[0], gp, (cc,))

            def mk_cs(l=l):
                def f(e):
                    last = None
                    for dch in range(DCH):
                        for nch in range(NCH):
                            last = e.matmul(psB[:, 1024 + dch:1024 + dch + 1],
                                            ws_sb[:, nch, l, dch * 128:(dch + 1) * 128],
                                            st4b[:, nch:nch + 1],
                                            start=(nch == 0), stop=(nch == NCH - 1))
                    return last
                return f
            op("tensor", mk_cs(), pe, (m_hsp,), tag="cs")
            m_csm = pe.mark()
            op("vector", (lambda l=l: (lambda e: e.tensor_tensor(
                out=bvec[:], in0=psB[:, 1024:1024 + DCH], in1=bs_f[:, :, l], op=Alu.add)))(), vec, (m_csm,))
            if l == 0:
                op("vector", lambda e: e.tensor_tensor(out=bvec0[:], in0=bvec[:], in1=bo_f[:, :, 0], op=Alu.add), vec, (vec,))
                m_bv0 = vec.mark()
                for dch in range(DCH):
                    def mk_y0(dch=dch):
                        def f(e):
                            last = None
                            for tt in range(4):
                                for nch in range(NCH):
                                    last = e.matmul(psA[:, tt * 512:(tt + 1) * 512],
                                                    wo_sb[:, nch, 0, dch * 128:(dch + 1) * 128],
                                                    hsp[:, nch, tt * 512:(tt + 1) * 512],
                                                    start=(nch == 0), stop=(nch == NCH - 1))
                            return last
                        return f
                    op("tensor", mk_y0(), pe, (m_hsp,) if dch == 0 else (m_psA,), tag="y0")
                    m_y0 = pe.mark()
                    emit_wjob_maybe(3)
                    op("scalar", (lambda dch=dch: (lambda e: e.activation(
                        out=h1_fm[:, dch, :], in_=psA[:], func=AF.Identity,
                        bias=bvec0[:, dch].unsqueeze(1))))(), act, (m_y0, m_bv0))
                    m_psA = act.mark()
                m_rhs = m_psA
            else:
                def mk_csr(e):
                    last = None
                    for dh in range(2):
                        for nch in range(NCH):
                            last = e.matmul(psB[0:1, dh * 512:(dh + 1) * 512],
                                            st4b[:, nch:nch + 1],
                                            ws_sb[:, nch, 1, dh * 512:(dh + 1) * 512],
                                            start=(nch == 0), stop=(nch == NCH - 1))
                    return last
                op("tensor", mk_csr, pe, (m_hsp, m_psB), tag="csr")
                m_csr = pe.mark()
                op("scalar", lambda e: e.activation(out=rowf[:], in_=psB[0:1, 0:1024], func=AF.Identity), act, (m_csr,))
                op("vector", lambda e: e.tensor_tensor(out=r1row[:], in0=rowf[:], in1=brow[:], op=Alu.add), vec, (act,))

        emit_wjob_maybe(len(_wq_thunks))
        emit_wgather()

        if _STAGE >= 3 and "h1fm" in dbg:
            dbg_dump("h1fm", h1_fm[:].rearrange("p a t -> p (a t)"), (act,))

        ring_pre1 = [None, None]
        m_x1act = None
        m_cast = None
        m_absg = [None, None]
        m_qB = [None, None]
        m_xqev = None
        for tcx in range(TC + 2 if _STAGE >= 4 else 0):
          if tcx < TC:
            tc = tcx
            pbuf = tc % 2
            o0 = 0 if pbuf == 0 else 1024
            slot_w = [m for m in (ring_pre1[pbuf],) if m is not None]
            dma("sync", xres[:, pbuf, :], x_in[tc * 128:(tc + 1) * 128, :], sxr_sl[pbuf],
                tuple(slot_w) + ((vec, pe) if tc == 0 else ()))
            def mk_y1(tc=tc, o0=o0):
                def f(e):
                    last = None
                    for dh in range(2):
                        for nch in range(NCH):
                            e.matmul(psA[:, o0 + dh * 512:o0 + (dh + 1) * 512],
                                     hsp[:, nch, tc * 128:(tc + 1) * 128],
                                     wo_sb[:, nch, 1, dh * 512:(dh + 1) * 512],
                                     start=(nch == 0), stop=False)
                        last = e.matmul(psA[:, o0 + dh * 512:o0 + (dh + 1) * 512],
                                        ones1[:], r1row[:, dh * 512:(dh + 1) * 512],
                                        start=False, stop=True)
                    return last
                return f
            op("tensor", mk_y1(), pe, tuple(slot_w) + ((vec,) if tc == 0 else ()), tag="y1")
            m_y1 = pe.mark()
            op("vector", (lambda o0=o0, pbuf=pbuf: (lambda e: e.tensor_tensor(
                out=pre1[:], in0=psA[:, o0:o0 + 1024], in1=xres[:, pbuf, :],
                op=Alu.add)))(), vec,
               (m_y1, sxr_sl[pbuf].mark()) + ((m_x1act,) if m_x1act is not None else ()))
            ring_pre1[pbuf] = vec.mark()
            op("vector", lambda e: e.tensor_reduce(out=st1[:, 0:1], in_=pre1[:], axis=AX, op=Alu.add), vec, (vec,))
            op("scalar", lambda e: e.activation(out=sqscr[:], in_=pre1[:], func=AF.Square,
                                                accum_out=st1[:, 1:2]), act, (vec,))
            op("vector", lambda e: e.tensor_single_scalar(out=st1[:, 2:3], in_=st1[:, 0:1], scalar=1.0 / D, op=Alu.mult), vec, (vec, act))
            op("vector", lambda e: e.tensor_tensor(out=st1[:, 3:4], in0=st1[:, 2:3], in1=st1[:, 2:3], op=Alu.mult), vec, (vec,))
            op("vector", lambda e: e.scalar_tensor_tensor(out=st1[:, 4:5], in0=st1[:, 1:2], scalar=1.0 / D,
                                                          in1=st1[:, 3:4], op0=Alu.mult, op1=Alu.subtract), vec, (vec,))
            op("scalar", lambda e: e.activation(out=st1[:, 5:6], in_=st1[:, 4:5], func=AF.Ln, bias=EPS), act, (vec,))
            op("scalar", lambda e: e.activation(out=st1[:, 5:6], in_=st1[:, 5:6], func=AF.Exp, scale=-0.5), act, (act,))
            op("vector", lambda e: e.tensor_tensor(out=st1[:, 6:7], in0=st1[:, 2:3], in1=st1[:, 5:6], op=Alu.mult), vec, (act,))
            op("vector", lambda e: e.tensor_single_scalar(out=st1[:, 6:7], in_=st1[:, 6:7], scalar=-1.0, op=Alu.mult), vec, (vec,))
            op("scalar", lambda e: e.activation(out=x1f[:], in_=pre1[:], func=AF.Identity,
                                                scale=st1[:, 5:6], bias=st1[:, 6:7]), act,
               ((vec,) + ((m_cast,) if m_cast is not None else ())))
            m_x1act = act.mark()
            op("scalar", lambda e: e.activation(out=x1b[:], in_=x1f[:], func=AF.Identity), act, (s_x1b,))
            m_cast = act.mark()
            dma("sync", x1_d[tc * 128:(tc + 1) * 128, :], x1b[:], s_x1b, (m_cast,))
            au_p = n_au if pbuf == 0 else n_au2
            sg_p = n_sg if pbuf == 0 else n_sg2
            m_absg[pbuf] = emit_quant_A(x1f[:], D, au_p, sg_p, pbuf * 64, waits=(act,))
          if 1 <= tcx < TC + 1:
            tc = tcx - 1
            pbuf = tc % 2
            au_p = n_au if pbuf == 0 else n_au2
            sg_p = n_sg if pbuf == 0 else n_sg2
            dst_p = n_dst if pbuf == 0 else n_dst2
            m_qB[pbuf] = emit_quant_B(dst_p[:], D, au_p, sg_p, pbuf * 64, m_absg[pbuf])
          if tcx >= 2:
            tc = tcx - 2
            pbuf = tc % 2
            dst_p = n_dst if pbuf == 0 else n_dst2
            def mk_tq(tc=tc, dst_p=dst_p):
                def f(e):
                    last = None
                    for dch in range(DCH):
                        last = e.transpose(psT[:, dch * 128:(dch + 1) * 128],
                                           dst_p[:, dch * 128:(dch + 1) * 128], identb[:])
                    return last
                return f
            op("tensor", mk_tq(), pe, (m_qB[pbuf],) + ((m_xqev,) if m_xqev is not None else ()), tag="tq")
            m_tq = pe.mark()
            op("scalar", (lambda tc=tc: (lambda e: e.activation(
                out=xq_fm[:, :, tc * 128:(tc + 1) * 128],
                in_=psT[:].rearrange("p (a t) -> p a t", t=128), func=AF.Identity)))(), act, (m_tq,))
            m_xqev = act.mark()

        if _STAGE >= 4:
            dbg_dump("x1", x1_d, (gp, act, st))
            dbg_dump("xq", xq_fm[:].rearrange("p a t -> p (a t)"), (act,))

        ITS = [(0, 512), (512, 512), (1024, 512), (1536, 512), (2048, 512), (2560, 256)]
        ring_slotA = [None, None]
        ring_wld = [None, None]
        ring_dn = [None] * 3
        m_hrow_free = None
        m_pre2 = None
        ring_x1r = [None, None, None]
        m_evac = [None, None, None]
        m_prevq = [None, None]
        for tcx in range(TC + 1 if _STAGE >= 5 else 0):
            if tcx < TC:
                tc = tcx
                x1r_w = (ring_x1r[tc % 3],) if ring_x1r[tc % 3] is not None else ()
                dma("sync", x1r_s(tc % 3), x1_d[tc * 128:(tc + 1) * 128, :], sx1_sl[tc % 3],
                    x1r_w + ((vec, st, gp, s_x1b) if tc < 2 else ()))
                for iti, (i0, ilen) in enumerate(ITS):
                    wbuf = iti % 2
                    go = 0 if (iti % 2 == 0) else 1024
                    wld_w = (ring_wld[wbuf],) if ring_wld[wbuf] is not None else ()
                    dma("sync", wgst[:, wbuf, :, :],
                        gout_wg[iti * 128:(iti + 1) * 128, :].rearrange("p (a j) -> p a j", a=DCH),
                        sg_sl[wbuf], wld_w + ((cc,) if tc == 0 else ()))
                    dma("sync", wust[:, wbuf, :, :],
                        gout_wu[iti * 128:(iti + 1) * 128, :].rearrange("p (a j) -> p a j", a=DCH),
                        su_sl[wbuf], wld_w + ((cc,) if tc == 0 else ()))
                    m_wld = (sg_sl[wbuf].mark(), su_sl[wbuf].mark())
                    def mk_g(tc=tc, go=go, wbuf=wbuf, ilen=ilen):
                        def f(e):
                            last = None
                            for kk in range(DCH):
                                e.matmul(psA[:, go:go + ilen],
                                         xq_fm[:, kk, tc * 128:(tc + 1) * 128],
                                         wgst[:, wbuf, kk, 0:ilen],
                                         start=(kk == 0), stop=(kk == DCH - 1))
                            for kk in range(DCH):
                                last = e.matmul(psA[:, go + 512:go + 512 + ilen],
                                                xq_fm[:, kk, tc * 128:(tc + 1) * 128],
                                                wust[:, wbuf, kk, 0:ilen],
                                                start=(kk == 0), stop=(kk == DCH - 1))
                            return last
                        return f
                    slotA_w = ring_slotA[iti % 2] or ()
                    _gx = {"v": (vec,), "a": (act,), "w": (ws, sg_sl[wbuf], su_sl[wbuf]), "": ()}[_GEXTRA]
                    op("tensor", mk_g(), pe, m_wld + tuple(slotA_w) + _gx +
                       (((vec, act) if tc == 0 and iti < 2 else ())), tag="g")
                    m_gate = pe.mark()
                    sq0 = (iti % 2) * 512
                    sq_w = ring_slotA[iti % 2][1:] if ring_slotA[iti % 2] else ()
                    op("scalar", (lambda go=go, ilen=ilen, sq0=sq0: (lambda e: e.activation(
                        out=sqscr[:, sq0:sq0 + ilen], in_=psA[:, go:go + ilen], func=AF.Silu)))(), act,
                       (m_gate,) + tuple(sq_w))
                    m_silu = act.mark()
                    hrow_w = ()
                    if iti == 0 and m_hrow_free is not None:
                        hrow_w = m_hrow_free
                    op("vector", (lambda go=go, ilen=ilen, i0=i0, sq0=sq0: (lambda e: e.tensor_tensor(
                        out=hrow[:, i0:i0 + ilen], in0=sqscr[:, sq0:sq0 + ilen],
                        in1=psA[:, go + 512:go + 512 + ilen], op=Alu.mult)))(), vec,
                       (m_silu,) + tuple(hrow_w))
                    ring_slotA[iti % 2] = (m_silu, vec.mark())
                    ring_wld[wbuf] = m_gate
            if tcx >= 1:
                tc = tcx - 1
                qsrc = w_dst if tc % 2 == 0 else w_dst2
                for gi_, (grp, gn) in enumerate(((0, 8), (8, 8), (16, 6))):
                    def mk_th(grp=grp, gn=gn, qsrc=qsrc):
                        def f(e):
                            last = None
                            for j in range(gn):
                                last = e.transpose(psT[:, j * 128:(j + 1) * 128],
                                                   qsrc[:, (grp + j) * 128:(grp + j + 1) * 128], identb[:])
                            return last
                        return f
                    prev_evac = m_evac[(gi_ + 2) % 3]
                    op("tensor", mk_th(), pe, (m_prevq[tc % 2],) +
                       ((prev_evac,) if prev_evac is not None else ()) +
                       ((act,) if tc == 0 else ()), tag="th")
                    m_tp = pe.mark()
                    op("scalar", (lambda grp=grp, gn=gn: (lambda e: e.activation(
                        out=hq_fm[:, grp:grp + gn, :],
                        in_=psT[:, 0:gn * 128].rearrange("p (a t) -> p a t", t=128), func=AF.Identity)))(), act, (m_tp,))
                    m_evac[gi_] = act.mark()
            if tcx < TC:
                tc = tcx
                qdst = w_dst if tc % 2 == 0 else w_dst2
                m_hrow_free = emit_quant(hrow[:], qdst[:], I, WBUF,
                                         waits=(m_prevq[tc % 2],) if m_prevq[tc % 2] is not None else ())
                m_prevq[tc % 2] = vec.mark()
            if tcx >= 1:
                tc = tcx - 1
                for ich in range(ICH):
                    ring = ich % 3
                    dn_w = (ring_dn[ring],) if ring_dn[ring] is not None else ()
                    dma("gpsimd", wdst[:, ring, :], wdq2[ich * 128:(ich + 1) * 128, :], sd_sl[ring],
                        dn_w + ((gp,) if tc == 0 else ()))
                    m_wdl = sd_sl[ring].mark()
                    def mk_dn(tc=tc, ich=ich, ring=ring):
                        def f(e):
                            last = None
                            for dh in range(2):
                                last = e.matmul(psB[:, dh * 512:(dh + 1) * 512],
                                                hq_fm[:, ich, :],
                                                wdst[:, ring, dh * 512:(dh + 1) * 512],
                                                start=(ich == 0), stop=(ich == ICH - 1))
                            return last
                        return f
                    extra = ()
                    if ich == 0:
                        extra = tuple(m for m in m_evac if m is not None)
                        if m_pre2 is not None:
                            extra = extra + (m_pre2,)
                    op("tensor", mk_dn(), pe, (m_wdl,) + extra, tag="dn")
                    ring_dn[ring] = pe.mark()
                m_dn = pe.mark()
                op("vector", (lambda tc=tc: (lambda e: e.tensor_tensor(
                    out=pre2s[:], in0=psB[:, 0:1024], in1=x1r_s(tc % 3),
                    op=Alu.add)))(), vec, (m_dn, sx1_sl[tc % 3].mark(), act, st))
                m_pre2 = vec.mark()
                ring_x1r[tc % 3] = m_pre2
                op("vector", (lambda tc=tc: (lambda e: e.tensor_reduce(
                    out=sums2[:, tc:tc + 1], in_=pre2s[:], axis=AX, op=Alu.add)))(), vec, (vec,))
                op("scalar", (lambda tc=tc: (lambda e: e.activation(
                    out=sqscr[:], in_=pre2s[:], func=AF.Square,
                    accum_out=ssq2[:, tc:tc + 1])))(), act, (vec,))
                dma("gpsimd", pre2_d[tc * 128:(tc + 1) * 128, :], pre2s[:], st, (m_pre2,))

        if _STAGE >= 5 and "pre2" in dbg:
            dma("gpsimd", dbg["pre2"], pre2_d, gp, (st,))
        if _STAGE >= 5:
            dbg_dump("sums2", sums2[:], (vec,))
            dbg_dump("ssq2", ssq2[:], (act,))

        for tc in range(TC if _STAGE >= 6 else 0):
            pbuf = tc % 2
            dma("sync", pre2r[:, pbuf, :], pre2_d[tc * 128:(tc + 1) * 128, :], sp2_sl[pbuf], (act, st))
            op("vector", (lambda tc=tc: (lambda e: e.tensor_single_scalar(
                out=st1[:, 2:3], in_=sums2[:, tc:tc + 1], scalar=1.0 / D, op=Alu.mult)))(), vec, (ld, act))
            op("vector", lambda e: e.tensor_tensor(out=st1[:, 3:4], in0=st1[:, 2:3], in1=st1[:, 2:3], op=Alu.mult), vec, (vec,))
            op("vector", (lambda tc=tc: (lambda e: e.scalar_tensor_tensor(
                out=st1[:, 4:5], in0=ssq2[:, tc:tc + 1], scalar=1.0 / D,
                in1=st1[:, 3:4], op0=Alu.mult, op1=Alu.subtract)))(), vec, (vec,))
            op("scalar", lambda e: e.activation(out=st1[:, 5:6], in_=st1[:, 4:5], func=AF.Ln, bias=EPS), act, (vec,))
            op("scalar", lambda e: e.activation(out=st1[:, 5:6], in_=st1[:, 5:6], func=AF.Exp, scale=-0.5), act, (act,))
            op("vector", lambda e: e.tensor_tensor(out=st1[:, 6:7], in0=st1[:, 2:3], in1=st1[:, 5:6], op=Alu.mult), vec, (act,))
            op("vector", lambda e: e.tensor_single_scalar(out=st1[:, 6:7], in_=st1[:, 6:7], scalar=-1.0, op=Alu.mult), vec, (vec,))
            op("scalar", (lambda pbuf=pbuf: (lambda e: e.activation(
                out=outb[:, pbuf, :], in_=pre2r[:, pbuf, :], func=AF.Identity,
                scale=st1[:, 5:6], bias=st1[:, 6:7])))(), act, (vec, sp2_sl[pbuf], st))
            dma("sync", out[tc * 128:(tc + 1) * 128, :], outb[:, pbuf, :], st, (act,))

        if _STAGE < 6:
            for tc in range(TC):
                dma("sync", out[tc * 128:(tc + 1) * 128, 0:8], cp_sb2[:], st)

        @block.sync
        def _(sync):
            for f in b.q["sync"]:
                f(sync)
            sync.wait_ge(st.h, st.n)

        @block.scalar
        def _(scalar):
            for f in b.q["scalar"]:
                f(scalar)

        @block.vector
        def _(vector):
            for f in b.q["vector"]:
                f(vector)

        @block.tensor
        def _(tensor):
            for f in b.q["tensor"]:
                f(tensor)

        @block.gpsimd
        def _(g):
            for f in b.q["gpsimd"]:
                f(g)

    return nc


_nc_cache = {}


def kernel(L0_attn_shot_embeddings, L0_attn_Wd, L0_attn_bd, L0_attn_Wi, L0_attn_bi,
           L0_attn_A_log, L0_attn_Wo, L0_attn_bo, L0_attn_Ws, L0_attn_bs,
           L0_attn_gamma, L0_attn_beta, L0_ffn_w_gate, L0_ffn_w_up, L0_ffn_w_down):
    x = np.ascontiguousarray(np.asarray(L0_attn_shot_embeddings, dtype=np.float32))
    Wd = np.ascontiguousarray(np.asarray(L0_attn_Wd, dtype=np.float32))
    bd = np.ascontiguousarray(np.asarray(L0_attn_bd, dtype=np.float32))
    Wi = np.ascontiguousarray(np.asarray(L0_attn_Wi, dtype=np.float32))
    bi = np.ascontiguousarray(np.asarray(L0_attn_bi, dtype=np.float32))
    A_log = np.ascontiguousarray(np.asarray(L0_attn_A_log, dtype=np.float32))
    Wo = np.ascontiguousarray(np.asarray(L0_attn_Wo, dtype=np.float32))
    bo = np.ascontiguousarray(np.asarray(L0_attn_bo, dtype=np.float32))
    Ws = np.ascontiguousarray(np.asarray(L0_attn_Ws, dtype=np.float32))
    bs = np.ascontiguousarray(np.asarray(L0_attn_bs, dtype=np.float32))
    wg = np.ascontiguousarray(np.asarray(L0_ffn_w_gate, dtype=np.float32))
    wu = np.ascontiguousarray(np.asarray(L0_ffn_w_up, dtype=np.float32))
    wdn = np.ascontiguousarray(np.asarray(L0_ffn_w_down, dtype=np.float32))

    key = ("main", _STAGE, _SERIAL, _SERIAL2, _SERIAL3, _GEXTRA, _ILW, _NOW, tuple(_DEBUG))
    if key not in _nc_cache:
        _nc_cache[key] = build_nc()
    nc = _nc_cache[key]

    cpool = np.tile(np.array([[0.0, 1.0, -1.0, 0.5, -0.5, EPS, 0.0, 0.0]], dtype=np.float32), (128, 1))
    ident = np.eye(128, dtype=ml_dtypes.bfloat16)
    ones1 = np.ones((1, 128), dtype=ml_dtypes.bfloat16)

    wg_p = np.concatenate([wg, np.zeros((NC_ * 512 - I, D), np.float32)], axis=0)
    wu_p = np.concatenate([wu, np.zeros((NC_ * 512 - I, D), np.float32)], axis=0)

    in_maps = []
    for c in range(NC_):
        bidx, k = c // 4, c % 4
        msel = np.zeros((128, 4, 8), np.float32)
        for j in range(4):
            msel[:, j, bidx * 4 + j] = 1.0
        ksel = np.zeros((128, 4), np.float32)
        ksel[:, k] = 1.0
        m = {
            "x": np.ascontiguousarray(x[bidx, k * T:(k + 1) * T, :]),
            "wd": Wd, "wi": Wi, "wo": Wo, "ws": Ws,
            "alog": np.ascontiguousarray(A_log.reshape(NL, NCH, 128).transpose(2, 1, 0)),
            "bd": np.ascontiguousarray(bd.reshape(NL, NCH, 128).transpose(2, 1, 0)),
            "bi": np.ascontiguousarray(bi.reshape(NL, NCH, 128).transpose(2, 1, 0)),
            "bo": np.ascontiguousarray(bo.reshape(NL, DCH, 128).transpose(2, 1, 0)),
            "bs": np.ascontiguousarray(bs.reshape(NL, DCH, 128).transpose(2, 1, 0)),
            "borow": np.ascontiguousarray(bo[1:2, :]),
            "bsrow": np.ascontiguousarray(bs[1:2, :]),
            "wg_sl": np.ascontiguousarray(wg_p[c * 512:(c + 1) * 512, :]),
            "wu_sl": np.ascontiguousarray(wu_p[c * 512:(c + 1) * 512, :]),
            "wdn_sl": np.ascontiguousarray(wdn[c * 128:(c + 1) * 128, :]),
            "cpool": cpool, "ident": ident, "ones1": ones1,
            "msel": msel, "ksel": ksel,
        }
        in_maps.append(m)

    res = run_bass_kernel_spmd(nc, in_maps, core_ids=list(range(NC_)))
    outf = np.empty((2, 8192, D), dtype=np.float32)
    for c in range(NC_):
        bidx, k = c // 4, c % 4
        outf[bidx, k * T:(k + 1) * T, :] = np.asarray(res.results[c]["out"])
    if _DEBUG:
        kernel._dbg = [{n: np.asarray(res.results[c]["dbg_" + n], dtype=np.float32)
                        for n in _DEBUG} for c in range(NC_)]
    return outf
